# version 17
# baseline (speedup 1.0000x reference)
DEBUG = False
INTERLEAVE = True

import numpy as np
from contextlib import ExitStack

import concourse.bass as bass
import concourse.mybir as mybir
from concourse.bass_utils import run_bass_kernel_spmd

F32 = mybir.dt.float32
BF16 = mybir.dt.bfloat16
AF = mybir.ActivationFunctionType
ALU = mybir.AluOpType
AX = mybir.AxisListType

P = 128
D = 1024
DC = 8
NT = 16
KT = 32
E = 8
I = 2048
ICN = 16
TBN = 4
NSLOT = 640
NSC = 5
EPS = 1e-5
QSCALE = 1.0 / 32.0


def L_of(j):
    return 256 * (j + 1)


class Sem:
    def __init__(self, es, nc, name):
        self.h = es.enter_context(nc.semaphore(name))
        self.n = 0

    def inc(self, k=1):
        self.n += k
        return self.n


class Plan:
    ENGS = ("sync", "tensor", "vector", "scalar", "gpsimd")

    def __init__(self):
        self.q = {e: [] for e in self.ENGS}

    def em(self, eng, fn):
        self.q[eng].append(fn)

    def wait(self, eng, sem, val):
        if val > 0:
            self.q[eng].append(lambda EG, h=sem.h, v=val: EG.wait_ge(h, v))

    def run(self, nc):
        with nc.Block() as block:
            @block.sync
            def _(e):
                for f in self.q["sync"]:
                    f(e)

            @block.tensor
            def _(e):
                for f in self.q["tensor"]:
                    f(e)

            @block.vector
            def _(e):
                for f in self.q["vector"]:
                    f(e)

            @block.scalar
            def _(e):
                for f in self.q["scalar"]:
                    f(e)

            @block.gpsimd
            def _(e):
                for f in self.q["gpsimd"]:
                    f(e)
        self.q = {e: [] for e in self.ENGS}


def build_nc():
    nc = bass.Bass("TRN2", target_bir_lowering=False)

    xq = nc.dram_tensor("xq", [2048, D], F32, kind="ExternalInput")
    xkv = nc.dram_tensor("xkv", [4096, D], F32, kind="ExternalInput")
    wqT = nc.dram_tensor("wqT", [D, D], F32, kind="ExternalInput")
    wkT = nc.dram_tensor("wkT", [D, D], F32, kind="ExternalInput")
    wvT = nc.dram_tensor("wvT", [D, D], F32, kind="ExternalInput")
    bq = nc.dram_tensor("bq", [D], F32, kind="ExternalInput")
    bk = nc.dram_tensor("bk", [D], F32, kind="ExternalInput")
    bv = nc.dram_tensor("bv", [D], F32, kind="ExternalInput")
    wr = nc.dram_tensor("wr", [D, E], F32, kind="ExternalInput")
    br = nc.dram_tensor("br", [E], F32, kind="ExternalInput")
    w1 = nc.dram_tensor("w1", [E, D, I], F32, kind="ExternalInput")
    w3 = nc.dram_tensor("w3", [E, D, I], F32, kind="ExternalInput")
    w2 = nc.dram_tensor("w2", [E, I, D], F32, kind="ExternalInput")
    ident = nc.dram_tensor("ident", [P, P], F32, kind="ExternalInput")
    ones1 = nc.dram_tensor("ones1", [1, P], F32, kind="ExternalInput")
    masks = nc.dram_tensor("masks", [NT, P, 256], F32, kind="ExternalInput")
    triu = nc.dram_tensor("triu", [P, P], F32, kind="ExternalInput")
    onespp = nc.dram_tensor("onespp", [P, P], F32, kind="ExternalInput")
    siota = nc.dram_tensor("siota", [P, 640], F32, kind="ExternalInput")

    out = nc.dram_tensor("out", [2048, D], F32, kind="ExternalOutput")
    if DEBUG:
        dbg_x2 = nc.dram_tensor("dbg_x2", [2048, D], F32, kind="ExternalOutput")
        dbg_v = nc.dram_tensor("dbg_v", [4096, D], BF16, kind="ExternalOutput")
        dbg_q = nc.dram_tensor("dbg_q", [P, NT * DC * P], BF16, kind="ExternalOutput")
        dbg_oa = nc.dram_tensor("dbg_oa", [2048, D], F32, kind="ExternalOutput")
        dbg_c = nc.dram_tensor("dbg_c", [P, NT * E], F32, kind="ExternalOutput")
        dbg_lg = nc.dram_tensor("dbg_lg", [P, NT * E], F32, kind="ExternalOutput")

    v_hbm = nc.dram_tensor("v_hbm", [4096, D], BF16, kind="Internal")
    qT_hbm = nc.dram_tensor("qT_hbm", [P, NT * DC * P], BF16, kind="Internal")
    x2_hbm = nc.dram_tensor("x2_hbm", [2048, D], F32, kind="Internal")
    ln2T_hbm = nc.dram_tensor("ln2T_hbm", [P, DC * 2048], BF16, kind="Internal")
    oacc_hbm = nc.dram_tensor("oacc_hbm", [2048, D], F32, kind="Internal")
    w1b_hbm = nc.dram_tensor("w1b_hbm", [E, D, I], BF16, kind="Internal")
    w3b_hbm = nc.dram_tensor("w3b_hbm", [E, D, I], BF16, kind="Internal")
    w2b_hbm = nc.dram_tensor("w2b_hbm", [E, I, D], BF16, kind="Internal")

    es = ExitStack()
    with es:
        identb = es.enter_context(nc.sbuf_tensor("identb", [P, P], BF16))
        onesb = es.enter_context(nc.sbuf_tensor("onesb", [1, P], BF16))
        onesf = es.enter_context(nc.sbuf_tensor("onesf", [1, P], F32))
        stat = es.enter_context(nc.sbuf_tensor("stat", [P, 64], F32))
        bqb = es.enter_context(nc.sbuf_tensor("bqb", [P, E], F32))
        bkb = es.enter_context(nc.sbuf_tensor("bkb", [P, E], F32))
        bvrow = es.enter_context(nc.sbuf_tensor("bvrow", [1, D], BF16))
        brrow = es.enter_context(nc.sbuf_tensor("brrow", [1, E], F32))
        brbc = es.enter_context(nc.sbuf_tensor("brbc", [P, E], F32))
        wrh = es.enter_context(nc.sbuf_tensor("wrh", [P, DC * E], BF16))
        csb = es.enter_context(nc.sbuf_tensor("csb", [P, NT * E], F32))
        logit = es.enter_context(nc.sbuf_tensor("logit", [P, NT * E], F32))

        SL = Sem(es, nc, "SL")
        SG = Sem(es, nc, "SG")
        ST = Sem(es, nc, "ST")
        SV = Sem(es, nc, "SV")
        SS = Sem(es, nc, "SS")
        S_CONST = Sem(es, nc, "S_CONST")
        S_XIN = [Sem(es, nc, f"S_XIN{i}") for i in range(2)]
        S_QEV = Sem(es, nc, "S_QEV")
        S_VV = [Sem(es, nc, f"S_VV{i}") for i in range(2)]
        S_MSK = Sem(es, nc, "S_MSK")
        S_QTS = [Sem(es, nc, f"S_QTS{i}") for i in range(2)]
        S_XT = [Sem(es, nc, f"S_XT{i}") for i in range(2)]
        S_VB = [Sem(es, nc, f"S_VB{i}") for i in range(4)]
        S_X2S = [Sem(es, nc, f"S_X2S{i}") for i in range(2)]
        S_HIT = Sem(es, nc, "S_HIT")
        S_LT = [Sem(es, nc, f"S_LT{i}") for i in range(2)]
        S_W13 = Sem(es, nc, "S_W13")
        S_W2 = Sem(es, nc, "S_W2")
        S_YT = [Sem(es, nc, f"S_YT{i}") for i in range(2)]
        S_F4 = [Sem(es, nc, f"S_F4{i}") for i in range(2)]
        S_WQ = [Sem(es, nc, f"S_WQ{i}") for i in range(2)]
        S_CAST = Sem(es, nc, "S_CAST")
        S_WQ4 = [Sem(es, nc, f"S_WQ4_{i}") for i in range(4)]

        pl = Plan()
        em, wait = pl.em, pl.wait

        def dma(eng, sem, dst, src, **kw):
            t = sem.inc(16)
            em(eng, lambda EG, d=dst, s=src, h=sem.h, kw=kw: EG.dma_start(d, s, **kw).then_inc(h, 16))
            return t

        def op(eng, sem, fn, k=1):
            t = sem.inc(k)
            em(eng, lambda EG, f=fn, h=sem.h, k=k: f(EG).then_inc(h, k))
            return t

        def quiet(eng, fn):
            em(eng, fn)

        t_consts = dma("gpsimd", SG, identb[:], ident[:])
        dma("gpsimd", SG, onesb[:], ones1[:])
        dma("gpsimd", SG, bvrow[:], bv.ap().unsqueeze(0))
        t_wrh = dma("gpsimd", SG, wrh[:].rearrange("p (c e) -> p c e", e=E),
                    wr.ap().rearrange("(c p) e -> p c e", p=P))
        t_ldf = dma("sync", S_CONST, onesf[:], onesf_src := ones1[:])
        dma("sync", S_CONST, bqb[:], bq.ap().rearrange("(c p) -> p c", p=P), allow_slow_non_contiguous=True)
        dma("sync", S_CONST, bkb[:], bk.ap().rearrange("(c p) -> p c", p=P), allow_slow_non_contiguous=True)
        t_ldf = dma("sync", S_CONST, brrow[:], br.ap().unsqueeze(0))
        wait("vector", S_CONST, t_ldf)
        t_bq = op("vector", SV, lambda EG: EG.tensor_scalar_mul(bqb[:], bqb[:], QSCALE))

        with es.enter_context(ExitStack()) as es12:
            kT = es12.enter_context(nc.sbuf_tensor("kT", [P, DC * 4096], BF16))

            with ExitStack() as es1:
                wq_sb = es1.enter_context(nc.sbuf_tensor("wq_sb", [P, DC * D], BF16))
                wk_sb = es1.enter_context(nc.sbuf_tensor("wk_sb", [P, DC * D], BF16))
                wv_sb = es1.enter_context(nc.sbuf_tensor("wv_sb", [P, DC * D], BF16))
                xin = [es1.enter_context(nc.sbuf_tensor(f"xin{i}", [P, 4 * D], F32)) for i in range(2)]
                lnb = [es1.enter_context(nc.sbuf_tensor(f"lnb{i}", [P, 4 * D], BF16)) for i in range(2)]
                lnT = [es1.enter_context(nc.sbuf_tensor(f"lnT{i}", [P, DC * 512], BF16)) for i in range(2)]
                qTev = es1.enter_context(nc.sbuf_tensor("qTev", [P, DC * 512], BF16))
                vev = [es1.enter_context(nc.sbuf_tensor(f"vev{i}", [P, 512], BF16)) for i in range(2)]
                ptr = [es1.enter_context(nc.psum_tensor(f"ptr{i}", [P, D], BF16)) for i in range(2)]
                pk = [es1.enter_context(nc.psum_tensor(f"pk{i}", [P, 512], F32)) for i in range(2)]
                pv = [es1.enter_context(nc.psum_tensor(f"pv{i}", [P, 512], F32)) for i in range(2)]

                t_wq = dma("gpsimd", SG, wq_sb[:].rearrange("p (c e) -> p c e", e=D),
                           wqT.ap().rearrange("(c p) e -> p c e", p=P))
                t_wk = dma("gpsimd", SG, wk_sb[:].rearrange("p (c e) -> p c e", e=D),
                           wkT.ap().rearrange("(c p) e -> p c e", p=P))
                t_wv = dma("gpsimd", SG, wv_sb[:].rearrange("p (c e) -> p c e", e=D),
                           wvT.ap().rearrange("(c p) e -> p c e", p=P))
                sqs = es1.enter_context(nc.sbuf_tensor("sqs", [P, 2 * D], F32))
                SG_PRE = SG.n
                wait("vector", SG, SG_PRE)
                t_wq_scaled = op("vector", SV, lambda EG: EG.tensor_scalar_mul(wq_sb[:], wq_sb[:], QSCALE))

                xin_free = [0, 0]
                lnT_done = [0, 0]
                lnT_free = [0, 0]
                lnb_free = [0, 0]
                ptr_free = [0, 0]
                pk_free = [0, 0]
                pv_free = [0, 0]
                vev_free = [0, 0]
                qev_free = [0]
                norm_done = [0, 0]

                xin_ld = [0, 0]

                def emit_xin_load(gi, src_t, base):
                    par = gi % 2
                    wait("sync", SV, xin_free[par])
                    xin_ld[par] = dma("sync", S_XIN[par], xin[par][:].rearrange("p (t d) -> p t d", d=D),
                                      src_t.ap()[base:base + 512, :].rearrange("(t p) d -> p t d", p=P))

                def emit_ln_vs(gi):
                    par = gi % 2
                    wait("vector", S_XIN[par], xin_ld[par])
                    for t in range(4):
                        tv = op("vector", SV, lambda EG, t=t, par=par: EG.reduce_sum(
                            stat[:, t:t + 1], xin[par][:, t * D:(t + 1) * D], axis=AX.X, negate=True))
                    wait("vector", SV, tv)
                    for t in range(4):
                        tv = op("vector", SV, lambda EG, t=t: EG.tensor_scalar_mul(
                            stat[:, 4 + t:5 + t], stat[:, t:t + 1], 1.0 / D))
                    wait("scalar", SV, tv)
                    for t in range(4):
                        if t == 2:
                            wait("scalar", SS, ts_)
                        ts_ = op("scalar", SS, lambda EG, t=t, par=par: EG.activation(
                            sqs[:, (t % 2) * D:(t % 2 + 1) * D], xin[par][:, t * D:(t + 1) * D], AF.Square,
                            bias=stat[:, 4 + t:5 + t], accum_out=stat[:, 8 + t:9 + t]))
                    wait("vector", SS, ts_)
                    for t in range(4):
                        tv = op("vector", SV, lambda EG, t=t: EG.tensor_scalar(
                            stat[:, 12 + t:13 + t], stat[:, 8 + t:9 + t], 1.0 / D, EPS,
                            op0=ALU.mult, op1=ALU.add))
                    wait("scalar", SV, tv)
                    for t in range(4):
                        ts_ = op("scalar", SS, lambda EG, t=t: EG.activation(
                            stat[:, 16 + t:17 + t], stat[:, 12 + t:13 + t], AF.Sqrt))
                    wait("vector", SS, ts_)
                    for t in range(4):
                        tv = op("vector", SV, lambda EG, t=t: EG.reciprocal(
                            stat[:, 20 + t:21 + t], stat[:, 16 + t:17 + t]))
                    wait("vector", SV, tv)
                    wait("vector", ST, lnb_free[par])
                    for t in range(4):
                        tv = op("vector", SV, lambda EG, t=t, par=par: EG.tensor_scalar(
                            lnb[par][:, t * D:(t + 1) * D], xin[par][:, t * D:(t + 1) * D],
                            stat[:, 4 + t:5 + t], stat[:, 20 + t:21 + t],
                            op0=ALU.add, op1=ALU.mult))
                    xin_free[par] = tv
                    norm_done[par] = tv

                def emit_trans(gi):
                    par = gi % 2
                    wait("tensor", SV, norm_done[par])
                    for t in range(4):
                        tpar = t % 2
                        wait("tensor", SV, ptr_free[tpar])
                        for c in range(DC):
                            fn = lambda EG, t=t, c=c, tpar=tpar, par=par: EG.transpose(
                                ptr[tpar][:, c * P:(c + 1) * P],
                                lnb[par][:, t * D + c * P: t * D + (c + 1) * P], identb[:])
                            tt = op("tensor", ST, fn) if c == DC - 1 else quiet("tensor", lambda EG, f=fn: f(EG))
                        if t == 3:
                            lnb_free[par] = tt
                        wait("vector", ST, tt)
                        wait("vector", ST, lnT_free[par])
                        tv2 = op("vector", SV, lambda EG, t=t, tpar=tpar, par=par: EG.tensor_copy(
                            lnT[par][:].rearrange("p (c u) -> p c u", u=512)[:, :, t * P:(t + 1) * P],
                            ptr[tpar][:].rearrange("p (c u) -> p c u", u=P)))
                        ptr_free[tpar] = tv2
                    lnT_done[par] = tv2

                def emit_q_mm(gi, g, ecs, first, last):
                    par = gi % 2
                    if first:
                        wait("tensor", SV, lnT_done[par])
                        wait("scalar", S_QEV, qev_free[0])
                    last_ev = 0
                    for ec in ecs:
                        kpar = ec % 2
                        wait("tensor", SS, pk_free[kpar])
                        for dc in range(DC):
                            fn = lambda EG, ec=ec, dc=dc, kpar=kpar, par=par: EG.matmul(
                                pk[kpar][:], wq_sb[:, dc * D + ec * P: dc * D + (ec + 1) * P],
                                lnT[par][:, dc * 512:(dc + 1) * 512],
                                start=(dc == 0), stop=(dc == DC - 1))
                            if dc == DC - 1:
                                tt = op("tensor", ST, fn)
                            else:
                                quiet("tensor", lambda EG, f=fn: f(EG))
                        if ec == DC - 1:
                            lnT_free[par] = tt
                        wait("scalar", ST, tt)
                        ts_ = op("scalar", SS, lambda EG, ec=ec, kpar=kpar: EG.activation(
                            qTev[:, ec * 512:(ec + 1) * 512], pk[kpar][:], AF.Identity,
                            bias=bqb[:, ec:ec + 1]))
                        pk_free[kpar] = ts_
                        last_ev = ts_
                    if not last:
                        return
                    for jj in range(4):
                        tq = dma("scalar", S_QEV,
                                 qT_hbm.ap()[:, (4 * g + jj) * 1024:(4 * g + jj + 1) * 1024]
                                 .rearrange("p (c u) -> p c u", u=P),
                                 qTev[:].rearrange("p (c v) -> p c v", v=512)[:, :, jj * P:(jj + 1) * P])
                    qev_free[0] = tq

                def emit_kv_mmK(gi, g):
                    par = gi % 2
                    wait("tensor", SV, lnT_done[par])
                    for ec in range(DC):
                        kpar = ec % 2
                        wait("tensor", SS, pk_free[kpar])
                        for dc in range(DC):
                            fn = lambda EG, ec=ec, dc=dc, kpar=kpar, par=par: EG.matmul(
                                pk[kpar][:], wk_sb[:, dc * D + ec * P: dc * D + (ec + 1) * P],
                                lnT[par][:, dc * 512:(dc + 1) * 512],
                                start=(dc == 0), stop=(dc == DC - 1))
                            if dc == DC - 1:
                                tt = op("tensor", ST, fn)
                            else:
                                quiet("tensor", lambda EG, f=fn: f(EG))
                        wait("scalar", ST, tt)
                        ts_ = op("scalar", SS, lambda EG, ec=ec, g=g, kpar=kpar: EG.activation(
                            kT[:, ec * 4096 + g * 512: ec * 4096 + (g + 1) * 512], pk[kpar][:],
                            AF.Identity, bias=bkb[:, ec:ec + 1]))
                        pk_free[kpar] = ts_

                def emit_kv_mmV(gi, g):
                    par = gi % 2
                    for t in range(4):
                        for eh in range(2):
                            vpar = (t * 2 + eh) % 2
                            wait("tensor", SS, pv_free[vpar])
                            for dc in range(DC):
                                fn = lambda EG, t=t, eh=eh, dc=dc, vpar=vpar, par=par: EG.matmul(
                                    pv[vpar][:], lnT[par][:, dc * 512 + t * P: dc * 512 + (t + 1) * P],
                                    wv_sb[:, dc * D + eh * 512: dc * D + (eh + 1) * 512],
                                    start=(dc == 0), stop=False)
                                quiet("tensor", lambda EG, f=fn: f(EG))
                            tt = op("tensor", ST, lambda EG, eh=eh, vpar=vpar: EG.matmul(
                                pv[vpar][:], onesb[:], bvrow[:, eh * 512:(eh + 1) * 512],
                                start=False, stop=True))
                            if t == 3 and eh == 1:
                                lnT_free[par] = tt
                            wait("scalar", ST, tt)
                            wait("scalar", S_VV[vpar], vev_free[vpar])
                            ts_ = op("scalar", SS, lambda EG, vpar=vpar: EG.copy(vev[vpar][:], pv[vpar][:]))
                            pv_free[vpar] = ts_
                            tl = dma("scalar", S_VV[vpar], v_hbm.ap()[g * 512 + t * P: g * 512 + (t + 1) * P,
                                                                      eh * 512:(eh + 1) * 512], vev[vpar][:])
                            vev_free[vpar] = tl

                order = [("q", g) for g in range(4)] + [("kv", g) for g in range(8)]
                wait("scalar", S_CONST, t_ldf)
                wait("scalar", SV, t_bq)
                wait("tensor", SG, SG_PRE)
                wait("tensor", SV, t_wq_scaled)

                def mm_phase1(kind, g, gi):
                    if kind == "q":
                        emit_q_mm(gi, g, range(0, 4), True, False)
                    else:
                        emit_kv_mmK(gi, g)

                def mm_phase2(kind, g, gi):
                    if kind == "q":
                        emit_q_mm(gi, g, range(4, 8), False, True)
                    else:
                        emit_kv_mmV(gi, g)

                emit_xin_load(0, xq, 0)
                prev = None
                for gi, (kind, g) in enumerate(order):
                    if gi + 1 < len(order):
                        k2, g2 = order[gi + 1]
                        emit_xin_load(gi + 1, xq if k2 == "q" else xkv, 512 * g2)
                    emit_ln_vs(gi)
                    if prev is not None:
                        mm_phase1(*prev)
                    emit_trans(gi)
                    if prev is not None:
                        mm_phase2(*prev)
                    prev = (kind, g, gi)
                mm_phase1(*prev)
                mm_phase2(*prev)
                pl.run(nc)

            with ExitStack() as es2:
                vblk = [es2.enter_context(nc.sbuf_tensor(f"vblk{i}", [P, 8 * D], BF16)) for i in range(4)]
                xtile = [es2.enter_context(nc.sbuf_tensor(f"xtile{i}", [P, D], F32)) for i in range(2)]
                qts = [es2.enter_context(nc.sbuf_tensor(f"qts{i}", [P, DC * P], BF16)) for i in range(2)]
                psb = [es2.enter_context(nc.sbuf_tensor(f"psb{i}", [P, 1024], BF16)) for i in range(2)]
                pTs = [es2.enter_context(nc.sbuf_tensor(f"pTs{i}", [P, 1024], BF16)) for i in range(2)]
                oacc = es2.enter_context(nc.sbuf_tensor("oacc", [P, D], F32))
                x2s = [es2.enter_context(nc.sbuf_tensor(f"x2s{i}", [P, D], F32)) for i in range(2)]
                msks = es2.enter_context(nc.sbuf_tensor("msks", [P, NT * 256], F32))
                pss = [es2.enter_context(nc.psum_tensor(f"pss{i}", [P, 1024], F32)) for i in range(2)]
                pso = es2.enter_context(nc.psum_tensor("pso", [P, D], F32))
                psp = [es2.enter_context(nc.psum_tensor(f"psp{i}", [P, 1024], BF16)) for i in range(2)]

                t_msk = dma("sync", S_MSK, msks[:].rearrange("p (j k) -> p j k", k=256),
                            masks.ap().rearrange("j p k -> p j k"))
                for _eng in ("gpsimd", "vector", "scalar", "sync"):
                    wait(_eng, S_VV[0], S_VV[0].n)
                    wait(_eng, S_VV[1], S_VV[1].n)
                    wait(_eng, S_QEV, S_QEV.n)

                blocks = []
                for j in range(NT):
                    Lj = L_of(j)
                    nb = (Lj + 1023) // 1024
                    for b in range(nb):
                        kb = min(1024, Lj - 1024 * b)
                        blocks.append((j, b, kb, b == nb - 1))
                NB = len(blocks)

                slot_busy = [0, 0, 0, 0]
                slot_loaded = [0, 0, 0, 0]
                qts_free = [0, 0]
                qts_ld = [0, 0]
                exp_end = [0, 0]
                trans_end = [0, 0]
                ptev_end = [0, 0]
                av_end = [0, 0]
                o_prev = [0]
                sums_prev = [0]
                xt_free = [0, 0]
                x2s_free = [0, 0]
                x2st = [0] * NT
                xq_ld = [0] * NT

                def m_merged(i):
                    return 25 + (i % 2)

                qts_ld[0] = dma("sync", S_QTS[0], qts[0][:], qT_hbm.ap()[:, 0:1024])
                xq_ld[0] = dma("sync", S_XT[0], xtile[0][:], xq.ap()[0:P, :])

                def emit_scores(i):
                    j, b, kb, last = blocks[i]
                    sp = i % 2
                    jpar = j % 2
                    wait("tensor", SS, exp_end[sp])
                    if b == 0:
                        wait("tensor", S_QTS[jpar], qts_ld[jpar])
                    nkc = (kb + 511) // 512
                    for kc in range(nkc):
                        w_ = min(512, kb - 512 * kc)
                        for ec in range(DC):
                            fn = lambda EG, kc=kc, ec=ec, w_=w_, j=j, jpar=jpar, b=b, sp=sp: EG.matmul(
                                pss[sp][:, kc * 512: kc * 512 + w_],
                                qts[jpar][:, ec * P:(ec + 1) * P],
                                kT[:, ec * 4096 + b * 1024 + kc * 512: ec * 4096 + b * 1024 + kc * 512 + w_],
                                start=(ec == 0), stop=(ec == DC - 1))
                            if ec == DC - 1 and kc == nkc - 1:
                                t_sc = op("tensor", ST, fn)
                            else:
                                quiet("tensor", lambda EG, f=fn: f(EG))
                    if last:
                        qts_free[jpar] = t_sc
                    return t_sc

                def emit_softmax_head(i, t_sc):
                    j, b, kb, last = blocks[i]
                    sp = i % 2
                    wait("vector", ST, t_sc)
                    if last:
                        if j == 0:
                            wait("vector", S_MSK, t_msk)
                        tv = op("vector", SV, lambda EG, j=j, kb=kb, sp=sp: EG.tensor_tensor(
                            pss[sp][:, kb - 256:kb], pss[sp][:, kb - 256:kb],
                            msks[:, j * 256:(j + 1) * 256], op=ALU.add))
                        wait("vector", SV, tv)
                    mcol = m_merged(i)
                    if b == 0:
                        tv = op("vector", SV, lambda EG, kb=kb, mcol=mcol, sp=sp: EG.reduce_max(
                            stat[:, mcol:mcol + 1], pss[sp][:, 0:kb], axis=AX.X, negate=True))
                    else:
                        pcol = m_merged(i - 1)
                        tv = op("vector", SV, lambda EG, kb=kb, sp=sp: EG.reduce_max(
                            stat[:, 24:25], pss[sp][:, 0:kb], axis=AX.X, negate=True))
                        wait("vector", SV, tv)
                        tv = op("vector", SV, lambda EG, mcol=mcol, pcol=pcol: EG.tensor_tensor(
                            stat[:, mcol:mcol + 1], stat[:, pcol:pcol + 1], stat[:, 24:25], op=ALU.min))
                        wait("vector", SV, tv)
                        dcol = 27 + (i % 2) * 7
                        tv = op("vector", SV, lambda EG, mcol=mcol, pcol=pcol, dcol=dcol: EG.tensor_sub(
                            stat[:, dcol:dcol + 1], stat[:, mcol:mcol + 1], stat[:, pcol:pcol + 1]))
                    wait("scalar", SV, tv)
                    wait("scalar", ST, trans_end[sp])
                    bcol = 29 + (i % 2) * 6
                    t_e = op("scalar", SS, lambda EG, kb=kb, mcol=mcol, sp=sp, bcol=bcol: EG.activation(
                        psb[sp][:, 0:kb], pss[sp][:, 0:kb], AF.Exp, bias=stat[:, mcol:mcol + 1],
                        accum_out=stat[:, bcol:bcol + 1]))
                    if b > 0:
                        acol = 28 + (i % 2) * 5
                        dcol = 27 + (i % 2) * 7
                        t_e = op("scalar", SS, lambda EG, acol=acol, dcol=dcol: EG.activation(
                            stat[:, acol:acol + 1], stat[:, dcol:dcol + 1], AF.Exp))
                    exp_end[sp] = t_e
                    return t_e

                def emit_block_tail1(i):
                    j, b, kb, last = blocks[i]
                    sp = i % 2
                    wait("tensor", SS, exp_end[sp])
                    wait("tensor", SV, ptev_end[sp])
                    for kc2 in range(kb // P):
                        fn = lambda EG, kc2=kc2, sp=sp: EG.transpose(
                            psp[sp][:, kc2 * P:(kc2 + 1) * P], psb[sp][:, kc2 * P:(kc2 + 1) * P], identb[:])
                        if kc2 == kb // P - 1:
                            t_tr = op("tensor", ST, fn)
                        else:
                            quiet("tensor", lambda EG, f=fn: f(EG))
                    trans_end[sp] = t_tr
                    wait("vector", ST, t_tr)
                    wait("vector", ST, av_end[sp])
                    tv_p = op("vector", SV, lambda EG, kb=kb, sp=sp: EG.tensor_copy(
                        pTs[sp][:, 0:kb], psp[sp][:, 0:kb]))
                    ptev_end[sp] = tv_p

                def emit_block_tail2(i):
                    j, b, kb, last = blocks[i]
                    sp = i % 2
                    s = i % 4
                    wait("tensor", SV, ptev_end[sp])
                    wait("tensor", SV, o_prev[0])
                    wait("tensor", S_VB[s], slot_loaded[s])
                    nkc2 = kb // P
                    for dh in range(2):
                        for kc2 in range(nkc2):
                            fn = lambda EG, s=s, u=kc2, dh=dh, kc2=kc2, nkc2=nkc2, sp=sp: EG.matmul(
                                pso[:, dh * 512:(dh + 1) * 512],
                                pTs[sp][:, kc2 * P:(kc2 + 1) * P],
                                vblk[s][:, u * D + dh * 512: u * D + dh * 512 + 512],
                                start=(kc2 == 0), stop=(kc2 == nkc2 - 1))
                            if dh == 1 and kc2 == nkc2 - 1:
                                t_av = op("tensor", ST, fn)
                            else:
                                quiet("tensor", lambda EG, f=fn: f(EG))
                    slot_busy[s] = t_av
                    av_end[sp] = t_av
                    bcol = 29 + (i % 2) * 6
                    scol = 30 + (b % 2)
                    wait("vector", SS, exp_end[i % 2])
                    if b == 0:
                        tv_s = op("vector", SV, lambda EG, scol=scol, bcol=bcol: EG.tensor_copy(
                            stat[:, scol:scol + 1], stat[:, bcol:bcol + 1]))
                    else:
                        acol = 28 + (i % 2) * 5
                        pscol = 30 + ((b - 1) % 2)
                        wait("vector", SV, sums_prev[0])
                        tv_s = op("vector", SV, lambda EG, scol=scol, pscol=pscol, acol=acol, bcol=bcol:
                                  EG.scalar_tensor_tensor(stat[:, scol:scol + 1], stat[:, pscol:pscol + 1],
                                                          stat[:, acol:acol + 1], stat[:, bcol:bcol + 1],
                                                          op0=ALU.mult, op1=ALU.add))
                    sums_prev[0] = tv_s
                    wait("vector", ST, t_av)
                    if b == 0:
                        tv_o = op("vector", SV, lambda EG: EG.tensor_copy(oacc[:], pso[:]))
                    else:
                        acol = 28 + (i % 2) * 5
                        wait("vector", SV, o_prev[0])
                        tv_o = op("vector", SV, lambda EG, acol=acol: EG.scalar_tensor_tensor(
                            oacc[:], oacc[:], stat[:, acol:acol + 1], pso[:], op0=ALU.mult, op1=ALU.add))
                    o_prev[0] = tv_o
                    if last:
                        jpar = j % 2
                        fcol = 30 + (b % 2)
                        wait("vector", SV, tv_s)
                        tv = op("vector", SV, lambda EG, fcol=fcol: EG.reciprocal(
                            stat[:, 32:33], stat[:, fcol:fcol + 1]))
                        wait("vector", SV, tv)
                        wait("vector", S_XT[j % 2], xq_ld[j])
                        wait("vector", S_X2S[jpar], x2s_free[jpar])
                        tv_f = op("vector", SV, lambda EG, jpar=jpar: EG.scalar_tensor_tensor(
                            x2s[jpar][:], oacc[:], stat[:, 32:33], xtile[jpar][:], op0=ALU.mult, op1=ALU.add))
                        xt_free[jpar] = tv_f
                        o_prev[0] = tv_f
                        wait("sync", SV, tv_f)
                        x2s_free[jpar] = dma("sync", S_X2S[jpar], x2_hbm.ap()[j * P:(j + 1) * P, :], x2s[jpar][:])
                        x2st[j] = x2s_free[jpar]

                for i in range(NB):
                    j, b, kb, last = blocks[i]
                    s = i % 4
                    wait("gpsimd", ST, slot_busy[s])
                    slot_loaded[s] = dma("gpsimd", S_VB[s],
                                         vblk[s][:].rearrange("p (u e) -> p u e", e=D)[:, 0:kb // P, :],
                                         v_hbm.ap()[b * 1024: b * 1024 + kb, :]
                                         .rearrange("(u p) e -> p u e", p=P))
                    t_sc = emit_scores(i)
                    if INTERLEAVE:
                        if i > 0:
                            emit_block_tail1(i - 1)
                        emit_softmax_head(i, t_sc)
                        if i > 0:
                            emit_block_tail2(i - 1)
                    else:
                        emit_softmax_head(i, t_sc)
                        emit_block_tail1(i)
                        emit_block_tail2(i)
                    if b == 0 and j + 1 < NT:
                        npar = (j + 1) % 2
                        wait("sync", ST, qts_free[npar])
                        qts_ld[npar] = dma("sync", S_QTS[npar], qts[npar][:],
                                           qT_hbm.ap()[:, (j + 1) * 1024:(j + 2) * 1024])
                        wait("sync", SV, xt_free[npar])
                        xq_ld[j + 1] = dma("sync", S_XT[npar], xtile[npar][:],
                                           xq.ap()[(j + 1) * P:(j + 2) * P, :])
                if INTERLEAVE:
                    emit_block_tail1(NB - 1)
                    emit_block_tail2(NB - 1)
                pl.run(nc)
            es12.close()

            with ExitStack() as es3:
                w1s = es3.enter_context(nc.sbuf_tensor("w1s", [P, DC * I], BF16))
                w3s = es3.enter_context(nc.sbuf_tensor("w3s", [P, DC * I], BF16))
                w2s = es3.enter_context(nc.sbuf_tensor("w2s", [P, ICN * D], BF16))
                ln2T = es3.enter_context(nc.sbuf_tensor("ln2T", [P, DC * 2048], BF16))
                hsb = [es3.enter_context(nc.sbuf_tensor(f"hsb{i}", [P, ICN * 512], BF16)) for i in range(2)]
                sils = [es3.enter_context(nc.sbuf_tensor(f"sils{i}", [P, 512], F32)) for i in range(2)]
                ytmp = [es3.enter_context(nc.sbuf_tensor(f"ytmp{i}", [P, D], F32)) for i in range(2)]
                sqs = es3.enter_context(nc.sbuf_tensor("sqs3", [P, D], F32))
                lnb3 = es3.enter_context(nc.sbuf_tensor("lnb3", [P, 2 * D], BF16))
                x4 = [es3.enter_context(nc.sbuf_tensor(f"x4_{i}", [P, 2 * D], F32)) for i in range(2)]
                cst = es3.enter_context(nc.sbuf_tensor("cst", [P, 112], F32))

                for _eng in ("vector", "scalar", "sync", "gpsimd"):
                    wait(_eng, S_X2S[0], S_X2S[0].n)
                    wait(_eng, S_X2S[1], S_X2S[1].n)

                with ExitStack() as es3a:
                    ptr3 = es3a.enter_context(nc.psum_tensor("ptr3", [P, D], BF16))
                    psr = es3a.enter_context(nc.psum_tensor("psr", [P, E], F32))

                    wait("tensor", S_CONST, t_ldf)
                    t_br = op("tensor", ST, lambda EG: EG.matmul(psr[:], onesf[:], brrow[:], start=True, stop=True))
                    wait("vector", ST, t_br)
                    t_brv = op("vector", SV, lambda EG: EG.tensor_copy(brbc[:], psr[:]))

                    x4_free = [0, 0]
                    lnb_busy = 0
                    tr_free = 0
                    router_done = t_brv
                    lnT_evac = 0
                    lnT_tile_done = [0] * NT
                    for g in range(8):
                        par = g % 2
                        wait("sync", SV, x4_free[par])
                        wait("sync", S_X2S[(2 * g) % 2], x2st[2 * g])
                        wait("sync", S_X2S[(2 * g + 1) % 2], x2st[2 * g + 1])
                        tl = dma("sync", S_LT[par], x4[par][:].rearrange("p (t d) -> p t d", d=D),
                                 x2_hbm.ap()[2 * g * P:(2 * g + 2) * P, :].rearrange("(t p) d -> p t d", p=P))
                        wait("vector", S_LT[par], tl)
                        for t in range(2):
                            tv = op("vector", SV, lambda EG, t=t, par=par: EG.reduce_sum(
                                cst[:, t:t + 1], x4[par][:, t * D:(t + 1) * D], axis=AX.X, negate=True))
                        wait("vector", SV, tv)
                        for t in range(2):
                            tv = op("vector", SV, lambda EG, t=t: EG.tensor_scalar_mul(
                                cst[:, 4 + t:5 + t], cst[:, t:t + 1], 1.0 / D))
                        wait("scalar", SV, tv)
                        for t in range(2):
                            if t == 1:
                                wait("scalar", SS, ts_)
                            ts_ = op("scalar", SS, lambda EG, t=t, par=par: EG.activation(
                                sqs[:], x4[par][:, t * D:(t + 1) * D], AF.Square,
                                bias=cst[:, 4 + t:5 + t], accum_out=cst[:, 8 + t:9 + t]))
                        wait("vector", SS, ts_)
                        for t in range(2):
                            tv = op("vector", SV, lambda EG, t=t: EG.tensor_scalar(
                                cst[:, 12 + t:13 + t], cst[:, 8 + t:9 + t], 1.0 / D, EPS,
                                op0=ALU.mult, op1=ALU.add))
                        wait("scalar", SV, tv)
                        for t in range(2):
                            ts_ = op("scalar", SS, lambda EG, t=t: EG.activation(
                                cst[:, 16 + t:17 + t], cst[:, 12 + t:13 + t], AF.Sqrt))
                        wait("vector", SS, ts_)
                        for t in range(2):
                            tv = op("vector", SV, lambda EG, t=t: EG.reciprocal(
                                cst[:, 20 + t:21 + t], cst[:, 16 + t:17 + t]))
                        wait("vector", SV, tv)
                        wait("vector", ST, lnb_busy)
                        for t in range(2):
                            tv = op("vector", SV, lambda EG, t=t, par=par: EG.tensor_scalar(
                                lnb3[:, t * D:(t + 1) * D], x4[par][:, t * D:(t + 1) * D],
                                cst[:, 4 + t:5 + t], cst[:, 20 + t:21 + t], op0=ALU.add, op1=ALU.mult))
                        x4_free[par] = tv
                        norm2 = tv
                        wait("tensor", SV, norm2)
                        for t in range(2):
                            j = 2 * g + t
                            wait("tensor", SV, tr_free)
                            for c in range(DC):
                                fn = lambda EG, t=t, c=c: EG.transpose(
                                    ptr3[:, c * P:(c + 1) * P],
                                    lnb3[:, t * D + c * P: t * D + (c + 1) * P], identb[:])
                                if c == DC - 1:
                                    t_tr = op("tensor", ST, fn)
                                else:
                                    quiet("tensor", lambda EG, f=fn: f(EG))
                            if t == 1:
                                lnb_busy = t_tr
                            wait("vector", ST, t_tr)
                            tv = op("vector", SV, lambda EG, j=j: EG.tensor_copy(
                                ln2T[:].rearrange("p (c v) -> p c v", v=2048)[:, :, j * P:(j + 1) * P],
                                ptr3[:].rearrange("p (c u) -> p c u", u=P)))
                            tr_free = tv
                            lnT_evac = tv
                            lnT_tile_done[j] = tv
                            wait("tensor", SV, tv)
                            wait("tensor", SG, SG_PRE)
                            wait("tensor", SV, router_done)
                            for dc in range(DC):
                                fn = lambda EG, dc=dc, j=j: EG.matmul(
                                    psr[:], ln2T[:, dc * 2048 + j * P: dc * 2048 + (j + 1) * P],
                                    wrh[:, dc * E:(dc + 1) * E],
                                    start=(dc == 0), stop=(dc == DC - 1))
                                if dc == DC - 1:
                                    t_r = op("tensor", ST, fn)
                                else:
                                    quiet("tensor", lambda EG, f=fn: f(EG))
                            wait("vector", ST, t_r)
                            wait("vector", SV, t_brv)
                            tv = op("vector", SV, lambda EG, j=j: EG.tensor_add(
                                logit[:, j * E:(j + 1) * E], psr[:], brbc[:]))
                            router_done = tv

                    wait("vector", SV, router_done)
                    for j in range(NT):
                        tv = op("vector", SV, lambda EG, j=j: EG.reduce_max(
                            cst[:, 32 + j:33 + j], logit[:, j * E:(j + 1) * E], axis=AX.X))
                    wait("vector", SV, tv)
                    tv = op("vector", SV, lambda EG: EG.tensor_scalar(
                        cst[:, 48:64], cst[:, 32:48], -1.0, None, op0=ALU.mult))
                    wait("vector", SV, tv)
                    for j in range(NT):
                        tv = op("vector", SV, lambda EG, j=j: EG.tensor_scalar(
                            csb[:, j * E:(j + 1) * E], logit[:, j * E:(j + 1) * E],
                            cst[:, 32 + j:33 + j], -1e30, op0=ALU.is_equal, op1=ALU.mult))
                    wait("vector", SV, tv)
                    for j in range(NT):
                        tv = op("vector", SV, lambda EG, j=j: EG.tensor_add(
                            csb[:, j * E:(j + 1) * E], csb[:, j * E:(j + 1) * E],
                            logit[:, j * E:(j + 1) * E]))
                    wait("vector", SV, tv)
                    for j in range(NT):
                        tv = op("vector", SV, lambda EG, j=j: EG.reduce_max(
                            cst[:, 64 + j:65 + j], csb[:, j * E:(j + 1) * E], axis=AX.X))
                    wait("vector", SV, tv)
                    tv_d = op("vector", SV, lambda EG: EG.tensor_sub(cst[:, 80:96], cst[:, 64:80], cst[:, 32:48]))
                    wait("scalar", SV, tv_d)
                    for j in range(NT):
                        ts_ = op("scalar", SS, lambda EG, j=j: EG.activation(
                            sqs[:, j * E:(j + 1) * E], logit[:, j * E:(j + 1) * E], AF.Exp,
                            bias=cst[:, 48 + j:49 + j]))
                    ts_d = op("scalar", SS, lambda EG: EG.activation(cst[:, 80:96], cst[:, 80:96], AF.Exp))
                    wait("vector", SS, ts_d)
                    tv = op("vector", SV, lambda EG: EG.tensor_scalar(
                        cst[:, 80:96], cst[:, 80:96], 1.0, None, op0=ALU.add))
                    wait("vector", SV, tv)
                    tv = op("vector", SV, lambda EG: EG.reciprocal(cst[:, 96:112], cst[:, 80:96]))
                    wait("vector", SV, tv)
                    for j in range(NT):
                        tv = op("vector", SV, lambda EG, j=j: EG.tensor_scalar(
                            csb[:, j * E:(j + 1) * E], logit[:, j * E:(j + 1) * E],
                            cst[:, 64 + j:65 + j], None, op0=ALU.is_ge))
                    wait("vector", SV, tv)
                    for j in range(NT):
                        tv = op("vector", SV, lambda EG, j=j: EG.tensor_mul(
                            csb[:, j * E:(j + 1) * E], csb[:, j * E:(j + 1) * E], sqs[:, j * E:(j + 1) * E]))
                    wait("vector", SV, tv)
                    for j in range(NT):
                        tv = op("vector", SV, lambda EG, j=j: EG.tensor_scalar(
                            csb[:, j * E:(j + 1) * E], csb[:, j * E:(j + 1) * E],
                            cst[:, 96 + j:97 + j], None, op0=ALU.mult))
                    c_done = tv

                    es3b = ExitStack()
                    psg = [es3b.enter_context(nc.psum_tensor(f"psg{i}", [P, 512], F32)) for i in range(2)]
                    psu = [es3b.enter_context(nc.psum_tensor(f"psu{i}", [P, 512], F32)) for i in range(2)]
                    psy = [es3b.enter_context(nc.psum_tensor(f"psy{i}", [P, 512], F32)) for i in range(2)]

                    def load_w13_quarter(e, q):
                        dma("gpsimd", S_WQ4[q],
                            w1s[:].rearrange("p (c i) -> p c i", i=I)[:, :, q * 512:(q + 1) * 512],
                            w1.ap()[e][:, q * 512:(q + 1) * 512].rearrange("(c p) i -> p c i", p=P))
                        return dma("gpsimd", S_WQ4[q],
                                   w3s[:].rearrange("p (c i) -> p c i", i=I)[:, :, q * 512:(q + 1) * 512],
                                   w3.ap()[e][:, q * 512:(q + 1) * 512].rearrange("(c p) i -> p c i", p=P))

                    def load_w2(e):
                        return dma("gpsimd", S_W2, w2s[:].rearrange("p (c d) -> p c d", d=D),
                                   w2.ap()[e].rearrange("(c p) d -> p c d", p=P))

                    wq13 = [load_w13_quarter(0, q) for q in range(4)]
                    t_w2 = load_w2(0)

                    psg_free = [0, 0]
                    psu_free = [0, 0]
                    psy_free = [0, 0]
                    sil_free = [0, 0]
                    ytmp_free = [0, 0]
                    h_ready = 0
                    w2_lastp = [0, 0]
                    gate_end = [0] * E
                    w2_end = [0] * E
                    acc_tile = [0] * NT

                    x2_3b = [0] * NT
                    def load_x2_3b(j):
                        wait("sync", SV, x4_free[j % 2])
                        wait("sync", S_X2S[j % 2], x2st[j])
                        x2_3b[j] = dma("sync", S_XT[j % 2], x4[j % 2][:, 0:D],
                                       x2_hbm.ap()[j * P:(j + 1) * P, :])
                    load_x2_3b(0)
                    load_x2_3b(1)
                    q_mark = [0] * 4

                    def emit_w2_phase(e, tb):
                        nonlocal w2_lastp, w2_end, acc_tile, ytmp_free, psy_free, x4_free, t_w2
                        hpar = tb % 2
                        if tb == 0:
                            wait("tensor", S_W2, t_w2)
                        for ts4 in range(4):
                            tsg = 4 * tb + ts4
                            for dh in range(2):
                                ypar = (ts4 * 2 + dh) % 2
                                wait("tensor", SV, psy_free[ypar])
                                for ic in range(ICN):
                                    fn = lambda E_, ic=ic, ts4=ts4, dh=dh, ypar=ypar, hpar=hpar: E_.matmul(
                                        psy[ypar][:], hsb[hpar][:, ic * 512 + ts4 * P: ic * 512 + (ts4 + 1) * P],
                                        w2s[:, ic * D + dh * 512: ic * D + dh * 512 + 512],
                                        start=(ic == 0), stop=(ic == ICN - 1))
                                    if ic == ICN - 1:
                                        t_y = op("tensor", ST, fn)
                                    else:
                                        quiet("tensor", lambda E_, f=fn: f(E_))
                                if ts4 == 3 and dh == 1:
                                    w2_lastp[hpar] = t_y
                                    if tb == TBN - 1:
                                        w2_end[e] = t_y
                                wait("vector", ST, t_y)
                                if dh == 0:
                                    wait("vector", S_YT[ts4 % 2], ytmp_free[ts4 % 2])
                                if e == 0:
                                    if dh == 0:
                                        wait("vector", S_XT[tsg % 2], x2_3b[tsg])
                                    tv = op("vector", SV, lambda E_, tsg=tsg, dh=dh, ypar=ypar, ts4=ts4:
                                            E_.scalar_tensor_tensor(
                                                ytmp[ts4 % 2][:, dh * 512:(dh + 1) * 512],
                                                psy[ypar][:], csb[:, tsg * E: tsg * E + 1],
                                                x4[tsg % 2][:, dh * 512:(dh + 1) * 512],
                                                op0=ALU.mult, op1=ALU.add))
                                else:
                                    tv = op("vector", SV, lambda E_, tsg=tsg, dh=dh, e=e, ypar=ypar, ts4=ts4:
                                            E_.tensor_scalar(ytmp[ts4 % 2][:, dh * 512:(dh + 1) * 512],
                                                             psy[ypar][:], csb[:, tsg * E + e: tsg * E + e + 1],
                                                             None, op0=ALU.mult))
                                psy_free[ypar] = tv
                                if e == 0 and dh == 1:
                                    x4_free[tsg % 2] = tv
                                    if tsg + 2 < NT:
                                        load_x2_3b(tsg + 2)
                            wait("gpsimd", SV, tv)
                            tg = dma("gpsimd", S_YT[ts4 % 2], out.ap()[tsg * P:(tsg + 1) * P, :],
                                     ytmp[ts4 % 2][:],
                                     accum_op=(ALU.add if e > 0 else ALU.bypass))
                            ytmp_free[ts4 % 2] = tg
                            if e == E - 1:
                                acc_tile[tsg] = tg
                        if e + 1 < E and tb == TBN - 1:
                            wait("gpsimd", ST, w2_end[e])
                            t_w2 = load_w2(e + 1)

                    pending = None
                    for e in range(E):
                        for tb in range(TBN):
                            hpar = tb % 2
                            if e == 0:
                                wait("tensor", SV, lnT_tile_done[4 * tb + 3])
                            for ic in range(ICN):
                                gpar = ic % 2
                                if tb == 0 and ic % 4 == 0:
                                    wait("tensor", S_WQ4[ic // 4], wq13[ic // 4])
                                if ic == 2 and pending is not None:
                                    emit_w2_phase(*pending)
                                    pending = None
                                wait("tensor", SS, psg_free[gpar])
                                for dc in range(DC):
                                    fn = lambda E_, dc=dc, ic=ic, gpar=gpar, tb=tb: E_.matmul(
                                        psg[gpar][:], w1s[:, dc * I + ic * P: dc * I + (ic + 1) * P],
                                        ln2T[:, dc * 2048 + tb * 512: dc * 2048 + (tb + 1) * 512],
                                        start=(dc == 0), stop=(dc == DC - 1))
                                    if dc == DC - 1:
                                        t_g = op("tensor", ST, fn)
                                    else:
                                        quiet("tensor", lambda E_, f=fn: f(E_))
                                wait("tensor", SV, psu_free[gpar])
                                for dc in range(DC):
                                    fn = lambda E_, dc=dc, ic=ic, gpar=gpar, tb=tb: E_.matmul(
                                        psu[gpar][:], w3s[:, dc * I + ic * P: dc * I + (ic + 1) * P],
                                        ln2T[:, dc * 2048 + tb * 512: dc * 2048 + (tb + 1) * 512],
                                        start=(dc == 0), stop=(dc == DC - 1))
                                    if dc == DC - 1:
                                        t_u = op("tensor", ST, fn)
                                    else:
                                        quiet("tensor", lambda E_, f=fn: f(E_))
                                if tb == TBN - 1 and ic % 4 == 3:
                                    q_mark[ic // 4] = t_u
                                if ic == ICN - 1 and tb == TBN - 1:
                                    gate_end[e] = t_u
                                wait("scalar", ST, t_g)
                                wait("scalar", SV, sil_free[gpar])
                                t_s = op("scalar", SS, lambda E_, gpar=gpar: E_.activation(
                                    sils[gpar][:], psg[gpar][:], AF.Silu))
                                psg_free[gpar] = t_s
                                wait("vector", SS, t_s)
                                wait("vector", ST, t_u)
                                if ic == 0:
                                    wait("vector", ST, w2_lastp[hpar])
                                tv = op("vector", SV, lambda E_, ic=ic, gpar=gpar, hpar=hpar: E_.tensor_tensor(
                                    hsb[hpar][:, ic * 512:(ic + 1) * 512], sils[gpar][:], psu[gpar][:],
                                    op=ALU.mult))
                                psu_free[gpar] = tv
                                sil_free[gpar] = tv
                                h_ready = tv
                            wait_val = h_ready
                            pending = (e, tb)
                        if e + 1 < E:
                            nxt = []
                            for q in range(4):
                                wait("gpsimd", ST, q_mark[q])
                                nxt.append(load_w13_quarter(e + 1, q))
                            wq13 = nxt
                    wait("tensor", SV, h_ready)
                    emit_w2_phase(*pending)
                    if DEBUG:
                        wait("sync", SV, c_done)
                        dma("sync", SL, dbg_x2.ap(), x2_hbm.ap())
                        dma("sync", SL, dbg_v.ap(), v_hbm.ap())
                        dma("sync", SL, dbg_q.ap(), qT_hbm.ap())
                        dma("sync", SL, dbg_c.ap(), csb[:])
                        dma("sync", SL, dbg_lg.ap(), logit[:])
                    wait("sync", S_YT[0], S_YT[0].n)
                    wait("sync", S_YT[1], S_YT[1].n)
                    wait("sync", SL, SL.n)
                    pl.run(nc)
                    es3b.close()
    return nc


_NC_CACHE = {}


def _get_nc():
    if "nc" not in _NC_CACHE:
        _NC_CACHE["nc"] = build_nc()
    return _NC_CACHE["nc"]


def _make_masks(p):
    m = np.zeros((NT, P, 256), dtype=np.float32)
    for j in range(NT):
        Lj = 256 * (j + 1)
        q_glob = 128 * (2 * j + p) + np.arange(P)[:, None]
        k_glob = (Lj - 256) + np.arange(256)[None, :]
        m[j] = np.where(k_glob <= q_glob, 0.0, -1e9)
    return m


def kernel(L0_attn_X, L0_attn_Wq, L0_attn_bq, L0_attn_Wk, L0_attn_bk, L0_attn_Wv, L0_attn_bv,
           L0_ffn_Wr, L0_ffn_br, L0_ffn_W1, L0_ffn_W3, L0_ffn_W2):
    X = np.ascontiguousarray(np.asarray(L0_attn_X, dtype=np.float32))
    wqT = np.ascontiguousarray(np.asarray(L0_attn_Wq, dtype=np.float32).T)
    wkT = np.ascontiguousarray(np.asarray(L0_attn_Wk, dtype=np.float32).T)
    wvT = np.ascontiguousarray(np.asarray(L0_attn_Wv, dtype=np.float32).T)
    bq = np.ascontiguousarray(np.asarray(L0_attn_bq, dtype=np.float32))
    bk = np.ascontiguousarray(np.asarray(L0_attn_bk, dtype=np.float32))
    bv = np.ascontiguousarray(np.asarray(L0_attn_bv, dtype=np.float32))
    wr = np.ascontiguousarray(np.asarray(L0_ffn_Wr, dtype=np.float32))
    br = np.ascontiguousarray(np.asarray(L0_ffn_br, dtype=np.float32))
    w1 = np.ascontiguousarray(np.asarray(L0_ffn_W1, dtype=np.float32))
    w3 = np.ascontiguousarray(np.asarray(L0_ffn_W3, dtype=np.float32))
    w2 = np.ascontiguousarray(np.asarray(L0_ffn_W2, dtype=np.float32))

    ident = np.eye(P, dtype=np.float32)
    ones1 = np.ones((1, P), dtype=np.float32)
    triu = np.triu(np.ones((P, P), dtype=np.float32), 1)
    onespp = np.ones((P, P), dtype=np.float32)
    siota = np.broadcast_to(np.arange(640, dtype=np.float32)[None, :], (P, 640)).copy()

    nc = _get_nc()
    in_maps = []
    for c in range(8):
        n, p = c // 2, c % 2
        xkv = X[n]
        xq = np.ascontiguousarray(X[n].reshape(32, P, D)[p::2].reshape(2048, D))
        in_maps.append(dict(xq=xq, xkv=xkv, wqT=wqT, wkT=wkT, wvT=wvT, bq=bq, bk=bk, bv=bv,
                            wr=wr, br=br, w1=w1, w3=w3, w2=w2,
                            ident=ident, ones1=ones1, masks=_make_masks(p),
                            triu=triu, onespp=onespp, siota=siota))
    res = run_bass_kernel_spmd(nc, in_maps, core_ids=list(range(8)))
    out = np.empty((4, 4096, D), dtype=np.float32)
    for c in range(8):
        n, p = c // 2, c % 2
        out[n].reshape(32, P, D)[p::2] = res.results[c]["out"].reshape(NT, P, D)
    return out


if __name__ == "__main__":
    nc = build_nc()
    print("built ok")


# version 18
# speedup vs baseline: 1.1367x; 1.1367x over previous
DEBUG = False
INTERLEAVE = True

import numpy as np
from contextlib import ExitStack

import concourse.bass as bass
import concourse.mybir as mybir
from concourse.bass_utils import run_bass_kernel_spmd

F32 = mybir.dt.float32
BF16 = mybir.dt.bfloat16
AF = mybir.ActivationFunctionType
ALU = mybir.AluOpType
AX = mybir.AxisListType

P = 128
D = 1024
DC = 8
NT = 16
KT = 32
E = 8
I = 2048
ICN = 16
TBN = 4
NSLOT = 640
NSC = 5
EPS = 1e-5
QSCALE = 1.0 / 32.0


def L_of(j):
    return 256 * (j + 1)


class Sem:
    def __init__(self, es, nc, name):
        self.h = es.enter_context(nc.semaphore(name))
        self.n = 0

    def inc(self, k=1):
        self.n += k
        return self.n


class Plan:
    ENGS = ("sync", "tensor", "vector", "scalar", "gpsimd")

    def __init__(self):
        self.q = {e: [] for e in self.ENGS}

    def em(self, eng, fn):
        self.q[eng].append(fn)

    def wait(self, eng, sem, val):
        if val > 0:
            self.q[eng].append(lambda EG, h=sem.h, v=val: EG.wait_ge(h, v))

    def run(self, nc):
        with nc.Block() as block:
            @block.sync
            def _(e):
                for f in self.q["sync"]:
                    f(e)

            @block.tensor
            def _(e):
                for f in self.q["tensor"]:
                    f(e)

            @block.vector
            def _(e):
                for f in self.q["vector"]:
                    f(e)

            @block.scalar
            def _(e):
                for f in self.q["scalar"]:
                    f(e)

            @block.gpsimd
            def _(e):
                for f in self.q["gpsimd"]:
                    f(e)
        self.q = {e: [] for e in self.ENGS}


def build_nc():
    nc = bass.Bass("TRN2", target_bir_lowering=False)

    xq = nc.dram_tensor("xq", [2048, D], F32, kind="ExternalInput")
    xkv = nc.dram_tensor("xkv", [4096, D], F32, kind="ExternalInput")
    wqT = nc.dram_tensor("wqT", [D, D], F32, kind="ExternalInput")
    wkT = nc.dram_tensor("wkT", [D, D], F32, kind="ExternalInput")
    wvT = nc.dram_tensor("wvT", [D, D], F32, kind="ExternalInput")
    bq = nc.dram_tensor("bq", [D], F32, kind="ExternalInput")
    bk = nc.dram_tensor("bk", [D], F32, kind="ExternalInput")
    bv = nc.dram_tensor("bv", [D], F32, kind="ExternalInput")
    wr = nc.dram_tensor("wr", [D, E], F32, kind="ExternalInput")
    br = nc.dram_tensor("br", [E], F32, kind="ExternalInput")
    w1 = nc.dram_tensor("w1", [E, D, I], F32, kind="ExternalInput")
    w3 = nc.dram_tensor("w3", [E, D, I], F32, kind="ExternalInput")
    w2 = nc.dram_tensor("w2", [E, I, D], F32, kind="ExternalInput")
    ident = nc.dram_tensor("ident", [P, P], F32, kind="ExternalInput")
    ones1 = nc.dram_tensor("ones1", [1, P], F32, kind="ExternalInput")
    masks = nc.dram_tensor("masks", [NT, P, 256], F32, kind="ExternalInput")
    triu = nc.dram_tensor("triu", [P, P], F32, kind="ExternalInput")
    onespp = nc.dram_tensor("onespp", [P, P], F32, kind="ExternalInput")
    siota = nc.dram_tensor("siota", [P, 640], F32, kind="ExternalInput")

    out = nc.dram_tensor("out", [2048, D], F32, kind="ExternalOutput")
    if DEBUG:
        dbg_x2 = nc.dram_tensor("dbg_x2", [2048, D], F32, kind="ExternalOutput")
        dbg_v = nc.dram_tensor("dbg_v", [4096, D], BF16, kind="ExternalOutput")
        dbg_q = nc.dram_tensor("dbg_q", [P, NT * DC * P], BF16, kind="ExternalOutput")
        dbg_oa = nc.dram_tensor("dbg_oa", [2048, D], F32, kind="ExternalOutput")
        dbg_c = nc.dram_tensor("dbg_c", [P, NT * E], F32, kind="ExternalOutput")
        dbg_lg = nc.dram_tensor("dbg_lg", [P, NT * E], F32, kind="ExternalOutput")

    v_hbm = nc.dram_tensor("v_hbm", [4096, D], BF16, kind="Internal")
    qT_hbm = nc.dram_tensor("qT_hbm", [P, NT * DC * P], BF16, kind="Internal")
    x2_hbm = nc.dram_tensor("x2_hbm", [2048, D], F32, kind="Internal")
    ln2T_hbm = nc.dram_tensor("ln2T_hbm", [P, DC * 2048], BF16, kind="Internal")
    oacc_hbm = nc.dram_tensor("oacc_hbm", [2048, D], F32, kind="Internal")
    w1b_hbm = nc.dram_tensor("w1b_hbm", [E, D, I], BF16, kind="Internal")
    w3b_hbm = nc.dram_tensor("w3b_hbm", [E, D, I], BF16, kind="Internal")
    w2b_hbm = nc.dram_tensor("w2b_hbm", [E, I, D], BF16, kind="Internal")

    es = ExitStack()
    with es:
        identb = es.enter_context(nc.sbuf_tensor("identb", [P, P], BF16))
        onesb = es.enter_context(nc.sbuf_tensor("onesb", [1, P], BF16))
        onesf = es.enter_context(nc.sbuf_tensor("onesf", [1, P], F32))
        stat = es.enter_context(nc.sbuf_tensor("stat", [P, 64], F32))
        bqb = es.enter_context(nc.sbuf_tensor("bqb", [P, E], F32))
        bkb = es.enter_context(nc.sbuf_tensor("bkb", [P, E], F32))
        bvrow = es.enter_context(nc.sbuf_tensor("bvrow", [1, D], BF16))
        brrow = es.enter_context(nc.sbuf_tensor("brrow", [1, E], F32))
        brbc = es.enter_context(nc.sbuf_tensor("brbc", [P, E], F32))
        wrh = es.enter_context(nc.sbuf_tensor("wrh", [P, DC * E], BF16))
        csb = es.enter_context(nc.sbuf_tensor("csb", [P, NT * E], F32))
        logit = es.enter_context(nc.sbuf_tensor("logit", [P, NT * E], F32))

        SL = Sem(es, nc, "SL")
        SG = Sem(es, nc, "SG")
        ST = Sem(es, nc, "ST")
        SV = Sem(es, nc, "SV")
        SS = Sem(es, nc, "SS")
        S_CONST = Sem(es, nc, "S_CONST")
        S_XIN = [Sem(es, nc, f"S_XIN{i}") for i in range(2)]
        S_QEV = Sem(es, nc, "S_QEV")
        S_VV = [Sem(es, nc, f"S_VV{i}") for i in range(2)]
        S_MSK = Sem(es, nc, "S_MSK")
        S_QTS = [Sem(es, nc, f"S_QTS{i}") for i in range(2)]
        S_XT = [Sem(es, nc, f"S_XT{i}") for i in range(2)]
        S_VB = [Sem(es, nc, f"S_VB{i}") for i in range(4)]
        S_X2S = [Sem(es, nc, f"S_X2S{i}") for i in range(2)]
        S_HIT = Sem(es, nc, "S_HIT")
        S_LT = [Sem(es, nc, f"S_LT{i}") for i in range(2)]
        S_W13 = Sem(es, nc, "S_W13")
        S_W2 = Sem(es, nc, "S_W2")
        S_YT = [Sem(es, nc, f"S_YT{i}") for i in range(2)]
        S_F4 = [Sem(es, nc, f"S_F4{i}") for i in range(2)]
        S_WQ = [Sem(es, nc, f"S_WQ{i}") for i in range(2)]
        S_CAST = Sem(es, nc, "S_CAST")
        S_WQ4 = [Sem(es, nc, f"S_WQ4_{i}") for i in range(4)]

        pl = Plan()
        em, wait = pl.em, pl.wait

        def dma(eng, sem, dst, src, **kw):
            t = sem.inc(16)
            em(eng, lambda EG, d=dst, s=src, h=sem.h, kw=kw: EG.dma_start(d, s, **kw).then_inc(h, 16))
            return t

        def op(eng, sem, fn, k=1):
            t = sem.inc(k)
            em(eng, lambda EG, f=fn, h=sem.h, k=k: f(EG).then_inc(h, k))
            return t

        def quiet(eng, fn):
            em(eng, fn)

        t_consts = dma("gpsimd", SG, identb[:], ident[:])
        dma("gpsimd", SG, onesb[:], ones1[:])
        dma("gpsimd", SG, bvrow[:], bv.ap().unsqueeze(0))
        t_wrh = dma("gpsimd", SG, wrh[:].rearrange("p (c e) -> p c e", e=E),
                    wr.ap().rearrange("(c p) e -> p c e", p=P))
        t_ldf = dma("sync", S_CONST, onesf[:], onesf_src := ones1[:])
        dma("sync", S_CONST, bqb[:], bq.ap().rearrange("(c p) -> p c", p=P), allow_slow_non_contiguous=True)
        dma("sync", S_CONST, bkb[:], bk.ap().rearrange("(c p) -> p c", p=P), allow_slow_non_contiguous=True)
        t_ldf = dma("sync", S_CONST, brrow[:], br.ap().unsqueeze(0))
        wait("vector", S_CONST, t_ldf)
        t_bq = op("vector", SV, lambda EG: EG.tensor_scalar_mul(bqb[:], bqb[:], QSCALE))

        with es.enter_context(ExitStack()) as es12:
            kT = es12.enter_context(nc.sbuf_tensor("kT", [P, DC * 4096], BF16))

            with ExitStack() as es1:
                wq_sb = es1.enter_context(nc.sbuf_tensor("wq_sb", [P, DC * D], BF16))
                wk_sb = es1.enter_context(nc.sbuf_tensor("wk_sb", [P, DC * D], BF16))
                wv_sb = es1.enter_context(nc.sbuf_tensor("wv_sb", [P, DC * D], BF16))
                xin = [es1.enter_context(nc.sbuf_tensor(f"xin{i}", [P, 4 * D], F32)) for i in range(2)]
                lnb = [es1.enter_context(nc.sbuf_tensor(f"lnb{i}", [P, 4 * D], BF16)) for i in range(2)]
                lnT = [es1.enter_context(nc.sbuf_tensor(f"lnT{i}", [P, DC * 512], BF16)) for i in range(2)]
                qTev = es1.enter_context(nc.sbuf_tensor("qTev", [P, DC * 512], BF16))
                vev = [es1.enter_context(nc.sbuf_tensor(f"vev{i}", [P, 512], BF16)) for i in range(2)]
                ptr = [es1.enter_context(nc.psum_tensor(f"ptr{i}", [P, D], BF16)) for i in range(2)]
                pk = [es1.enter_context(nc.psum_tensor(f"pk{i}", [P, 512], F32)) for i in range(2)]
                pv = [es1.enter_context(nc.psum_tensor(f"pv{i}", [P, 512], F32)) for i in range(2)]

                t_wq = dma("gpsimd", SG, wq_sb[:].rearrange("p (c e) -> p c e", e=D),
                           wqT.ap().rearrange("(c p) e -> p c e", p=P))
                t_wk = dma("gpsimd", SG, wk_sb[:].rearrange("p (c e) -> p c e", e=D),
                           wkT.ap().rearrange("(c p) e -> p c e", p=P))
                t_wv = dma("gpsimd", SG, wv_sb[:].rearrange("p (c e) -> p c e", e=D),
                           wvT.ap().rearrange("(c p) e -> p c e", p=P))
                sqs = es1.enter_context(nc.sbuf_tensor("sqs", [P, 2 * D], F32))
                SG_PRE = SG.n
                wait("vector", SG, SG_PRE)
                t_wq_scaled = op("vector", SV, lambda EG: EG.tensor_scalar_mul(wq_sb[:], wq_sb[:], QSCALE))

                xin_free = [0, 0]
                lnT_done = [0, 0]
                lnT_free = [0, 0]
                lnb_free = [0, 0]
                ptr_free = [0, 0]
                pk_free = [0, 0]
                pv_free = [0, 0]
                vev_free = [0, 0]
                qev_free = [0]
                norm_done = [0, 0]

                xin_ld = [0, 0]

                def emit_xin_load(gi, src_t, base):
                    par = gi % 2
                    wait("sync", SV, xin_free[par])
                    xin_ld[par] = dma("sync", S_XIN[par], xin[par][:].rearrange("p (t d) -> p t d", d=D),
                                      src_t.ap()[base:base + 512, :].rearrange("(t p) d -> p t d", p=P))

                def emit_ln_vs(gi):
                    par = gi % 2
                    wait("vector", S_XIN[par], xin_ld[par])
                    for t in range(4):
                        tv = op("vector", SV, lambda EG, t=t, par=par: EG.reduce_sum(
                            stat[:, t:t + 1], xin[par][:, t * D:(t + 1) * D], axis=AX.X, negate=True))
                    wait("vector", SV, tv)
                    for t in range(4):
                        tv = op("vector", SV, lambda EG, t=t: EG.tensor_scalar_mul(
                            stat[:, 4 + t:5 + t], stat[:, t:t + 1], 1.0 / D))
                    wait("scalar", SV, tv)
                    for t in range(4):
                        if t == 2:
                            wait("scalar", SS, ts_)
                        ts_ = op("scalar", SS, lambda EG, t=t, par=par: EG.activation(
                            sqs[:, (t % 2) * D:(t % 2 + 1) * D], xin[par][:, t * D:(t + 1) * D], AF.Square,
                            bias=stat[:, 4 + t:5 + t], accum_out=stat[:, 8 + t:9 + t]))
                    wait("vector", SS, ts_)
                    for t in range(4):
                        tv = op("vector", SV, lambda EG, t=t: EG.tensor_scalar(
                            stat[:, 12 + t:13 + t], stat[:, 8 + t:9 + t], 1.0 / D, EPS,
                            op0=ALU.mult, op1=ALU.add))
                    wait("scalar", SV, tv)
                    for t in range(4):
                        ts_ = op("scalar", SS, lambda EG, t=t: EG.activation(
                            stat[:, 16 + t:17 + t], stat[:, 12 + t:13 + t], AF.Sqrt))
                    wait("vector", SS, ts_)
                    for t in range(4):
                        tv = op("vector", SV, lambda EG, t=t: EG.reciprocal(
                            stat[:, 20 + t:21 + t], stat[:, 16 + t:17 + t]))
                    wait("vector", SV, tv)
                    wait("vector", ST, lnb_free[par])
                    for t in range(4):
                        tv = op("vector", SV, lambda EG, t=t, par=par: EG.tensor_scalar(
                            lnb[par][:, t * D:(t + 1) * D], xin[par][:, t * D:(t + 1) * D],
                            stat[:, 4 + t:5 + t], stat[:, 20 + t:21 + t],
                            op0=ALU.add, op1=ALU.mult))
                    xin_free[par] = tv
                    norm_done[par] = tv

                def emit_trans(gi):
                    par = gi % 2
                    wait("tensor", SV, norm_done[par])
                    for t in range(4):
                        tpar = t % 2
                        wait("tensor", SV, ptr_free[tpar])
                        for c in range(DC):
                            fn = lambda EG, t=t, c=c, tpar=tpar, par=par: EG.transpose(
                                ptr[tpar][:, c * P:(c + 1) * P],
                                lnb[par][:, t * D + c * P: t * D + (c + 1) * P], identb[:])
                            tt = op("tensor", ST, fn) if c == DC - 1 else quiet("tensor", lambda EG, f=fn: f(EG))
                        if t == 3:
                            lnb_free[par] = tt
                        wait("vector", ST, tt)
                        wait("vector", ST, lnT_free[par])
                        tv2 = op("vector", SV, lambda EG, t=t, tpar=tpar, par=par: EG.tensor_copy(
                            lnT[par][:].rearrange("p (c u) -> p c u", u=512)[:, :, t * P:(t + 1) * P],
                            ptr[tpar][:].rearrange("p (c u) -> p c u", u=P)))
                        ptr_free[tpar] = tv2
                    lnT_done[par] = tv2

                def emit_q_mm(gi, g, ecs, first, last):
                    par = gi % 2
                    if first:
                        wait("tensor", SV, lnT_done[par])
                        wait("scalar", S_QEV, qev_free[0])
                    last_ev = 0
                    for ec in ecs:
                        kpar = ec % 2
                        wait("tensor", SS, pk_free[kpar])
                        for dc in range(DC):
                            fn = lambda EG, ec=ec, dc=dc, kpar=kpar, par=par: EG.matmul(
                                pk[kpar][:], wq_sb[:, dc * D + ec * P: dc * D + (ec + 1) * P],
                                lnT[par][:, dc * 512:(dc + 1) * 512],
                                start=(dc == 0), stop=(dc == DC - 1))
                            if dc == DC - 1:
                                tt = op("tensor", ST, fn)
                            else:
                                quiet("tensor", lambda EG, f=fn: f(EG))
                        if ec == DC - 1:
                            lnT_free[par] = tt
                        wait("scalar", ST, tt)
                        ts_ = op("scalar", SS, lambda EG, ec=ec, kpar=kpar: EG.activation(
                            qTev[:, ec * 512:(ec + 1) * 512], pk[kpar][:], AF.Identity,
                            bias=bqb[:, ec:ec + 1]))
                        pk_free[kpar] = ts_
                        last_ev = ts_
                    if not last:
                        return
                    wait("sync", SS, last_ev)
                    for jj in range(4):
                        tq = dma("sync", S_QEV,
                                 qT_hbm.ap()[:, (4 * g + jj) * 1024:(4 * g + jj + 1) * 1024]
                                 .rearrange("p (c u) -> p c u", u=P),
                                 qTev[:].rearrange("p (c v) -> p c v", v=512)[:, :, jj * P:(jj + 1) * P])
                    qev_free[0] = tq

                def emit_kv_mmK(gi, g):
                    par = gi % 2
                    wait("tensor", SV, lnT_done[par])
                    for ec in range(DC):
                        kpar = ec % 2
                        wait("tensor", SS, pk_free[kpar])
                        for dc in range(DC):
                            fn = lambda EG, ec=ec, dc=dc, kpar=kpar, par=par: EG.matmul(
                                pk[kpar][:], wk_sb[:, dc * D + ec * P: dc * D + (ec + 1) * P],
                                lnT[par][:, dc * 512:(dc + 1) * 512],
                                start=(dc == 0), stop=(dc == DC - 1))
                            if dc == DC - 1:
                                tt = op("tensor", ST, fn)
                            else:
                                quiet("tensor", lambda EG, f=fn: f(EG))
                        wait("scalar", ST, tt)
                        ts_ = op("scalar", SS, lambda EG, ec=ec, g=g, kpar=kpar: EG.activation(
                            kT[:, ec * 4096 + g * 512: ec * 4096 + (g + 1) * 512], pk[kpar][:],
                            AF.Identity, bias=bkb[:, ec:ec + 1]))
                        pk_free[kpar] = ts_

                def emit_kv_mmV(gi, g):
                    par = gi % 2
                    for t in range(4):
                        for eh in range(2):
                            vpar = (t * 2 + eh) % 2
                            wait("tensor", SS, pv_free[vpar])
                            for dc in range(DC):
                                fn = lambda EG, t=t, eh=eh, dc=dc, vpar=vpar, par=par: EG.matmul(
                                    pv[vpar][:], lnT[par][:, dc * 512 + t * P: dc * 512 + (t + 1) * P],
                                    wv_sb[:, dc * D + eh * 512: dc * D + (eh + 1) * 512],
                                    start=(dc == 0), stop=False)
                                quiet("tensor", lambda EG, f=fn: f(EG))
                            tt = op("tensor", ST, lambda EG, eh=eh, vpar=vpar: EG.matmul(
                                pv[vpar][:], onesb[:], bvrow[:, eh * 512:(eh + 1) * 512],
                                start=False, stop=True))
                            if t == 3 and eh == 1:
                                lnT_free[par] = tt
                            wait("scalar", ST, tt)
                            wait("scalar", S_VV[vpar], vev_free[vpar])
                            ts_ = op("scalar", SS, lambda EG, vpar=vpar: EG.copy(vev[vpar][:], pv[vpar][:]))
                            pv_free[vpar] = ts_
                            wait("sync", SS, ts_)
                            tl = dma("sync", S_VV[vpar], v_hbm.ap()[g * 512 + t * P: g * 512 + (t + 1) * P,
                                                                    eh * 512:(eh + 1) * 512], vev[vpar][:])
                            vev_free[vpar] = tl

                order = [("q", g) for g in range(4)] + [("kv", g) for g in range(8)]
                wait("scalar", S_CONST, t_ldf)
                wait("scalar", SV, t_bq)
                wait("tensor", SG, SG_PRE)
                wait("tensor", SV, t_wq_scaled)

                def mm_phase1(kind, g, gi):
                    if kind == "q":
                        emit_q_mm(gi, g, range(0, 4), True, False)
                    else:
                        emit_kv_mmK(gi, g)

                def mm_phase2(kind, g, gi):
                    if kind == "q":
                        emit_q_mm(gi, g, range(4, 8), False, True)
                    else:
                        emit_kv_mmV(gi, g)

                emit_xin_load(0, xq, 0)
                prev = None
                for gi, (kind, g) in enumerate(order):
                    if gi + 1 < len(order):
                        k2, g2 = order[gi + 1]
                        emit_xin_load(gi + 1, xq if k2 == "q" else xkv, 512 * g2)
                    emit_ln_vs(gi)
                    if prev is not None:
                        mm_phase1(*prev)
                    emit_trans(gi)
                    if prev is not None:
                        mm_phase2(*prev)
                    prev = (kind, g, gi)
                mm_phase1(*prev)
                mm_phase2(*prev)
                pl.run(nc)

            with ExitStack() as es2:
                vblk = [es2.enter_context(nc.sbuf_tensor(f"vblk{i}", [P, 8 * D], BF16)) for i in range(4)]
                xtile = [es2.enter_context(nc.sbuf_tensor(f"xtile{i}", [P, D], F32)) for i in range(2)]
                qts = [es2.enter_context(nc.sbuf_tensor(f"qts{i}", [P, DC * P], BF16)) for i in range(2)]
                psb = [es2.enter_context(nc.sbuf_tensor(f"psb{i}", [P, 1024], BF16)) for i in range(2)]
                pTs = [es2.enter_context(nc.sbuf_tensor(f"pTs{i}", [P, 1024], BF16)) for i in range(2)]
                oacc = es2.enter_context(nc.sbuf_tensor("oacc", [P, D], F32))
                x2s = [es2.enter_context(nc.sbuf_tensor(f"x2s{i}", [P, D], F32)) for i in range(2)]
                msks = es2.enter_context(nc.sbuf_tensor("msks", [P, NT * 256], F32))
                pss = [es2.enter_context(nc.psum_tensor(f"pss{i}", [P, 1024], F32)) for i in range(2)]
                pso = es2.enter_context(nc.psum_tensor("pso", [P, D], F32))
                psp = [es2.enter_context(nc.psum_tensor(f"psp{i}", [P, 1024], BF16)) for i in range(2)]

                t_msk = dma("gpsimd", S_MSK, msks[:].rearrange("p (j k) -> p j k", k=256),
                            masks.ap().rearrange("j p k -> p j k"))
                for _eng in ("gpsimd", "vector", "scalar", "sync"):
                    wait(_eng, S_VV[0], S_VV[0].n)
                    wait(_eng, S_VV[1], S_VV[1].n)
                    wait(_eng, S_QEV, S_QEV.n)

                blocks = []
                for j in range(NT):
                    Lj = L_of(j)
                    nb = (Lj + 1023) // 1024
                    for b in range(nb):
                        kb = min(1024, Lj - 1024 * b)
                        blocks.append((j, b, kb, b == nb - 1))
                NB = len(blocks)

                slot_busy = [0, 0, 0, 0]
                slot_loaded = [0, 0, 0, 0]
                qts_free = [0, 0]
                qts_ld = [0, 0]
                exp_end = [0, 0]
                trans_end = [0, 0]
                ptev_end = [0, 0]
                av_end = [0, 0]
                o_prev = [0]
                sums_prev = [0]
                xt_free = [0, 0]
                x2s_free = [0, 0]
                x2st = [0] * NT
                xq_ld = [0] * NT

                def m_merged(i):
                    return 25 + (i % 2)

                qts_ld[0] = dma("sync", S_QTS[0], qts[0][:], qT_hbm.ap()[:, 0:1024])
                xq_ld[0] = dma("sync", S_XT[0], xtile[0][:], xq.ap()[0:P, :])

                def emit_scores(i):
                    j, b, kb, last = blocks[i]
                    sp = i % 2
                    jpar = j % 2
                    wait("tensor", SS, exp_end[sp])
                    if b == 0:
                        wait("tensor", S_QTS[jpar], qts_ld[jpar])
                    nkc = (kb + 511) // 512
                    for kc in range(nkc):
                        w_ = min(512, kb - 512 * kc)
                        for ec in range(DC):
                            fn = lambda EG, kc=kc, ec=ec, w_=w_, j=j, jpar=jpar, b=b, sp=sp: EG.matmul(
                                pss[sp][:, kc * 512: kc * 512 + w_],
                                qts[jpar][:, ec * P:(ec + 1) * P],
                                kT[:, ec * 4096 + b * 1024 + kc * 512: ec * 4096 + b * 1024 + kc * 512 + w_],
                                start=(ec == 0), stop=(ec == DC - 1))
                            if ec == DC - 1 and kc == nkc - 1:
                                t_sc = op("tensor", ST, fn)
                            else:
                                quiet("tensor", lambda EG, f=fn: f(EG))
                    if last:
                        qts_free[jpar] = t_sc
                    return t_sc

                def emit_softmax_head(i, t_sc):
                    j, b, kb, last = blocks[i]
                    sp = i % 2
                    wait("vector", ST, t_sc)
                    if last:
                        if j == 0:
                            wait("vector", S_MSK, t_msk)
                        tv = op("vector", SV, lambda EG, j=j, kb=kb, sp=sp: EG.tensor_tensor(
                            pss[sp][:, kb - 256:kb], pss[sp][:, kb - 256:kb],
                            msks[:, j * 256:(j + 1) * 256], op=ALU.add))
                        wait("vector", SV, tv)
                    mcol = m_merged(i)
                    if b == 0:
                        tv = op("vector", SV, lambda EG, kb=kb, mcol=mcol, sp=sp: EG.reduce_max(
                            stat[:, mcol:mcol + 1], pss[sp][:, 0:kb], axis=AX.X, negate=True))
                    else:
                        pcol = m_merged(i - 1)
                        tv = op("vector", SV, lambda EG, kb=kb, sp=sp: EG.reduce_max(
                            stat[:, 24:25], pss[sp][:, 0:kb], axis=AX.X, negate=True))
                        wait("vector", SV, tv)
                        tv = op("vector", SV, lambda EG, mcol=mcol, pcol=pcol: EG.tensor_tensor(
                            stat[:, mcol:mcol + 1], stat[:, pcol:pcol + 1], stat[:, 24:25], op=ALU.min))
                        wait("vector", SV, tv)
                        dcol = 27 + (i % 2) * 7
                        tv = op("vector", SV, lambda EG, mcol=mcol, pcol=pcol, dcol=dcol: EG.tensor_sub(
                            stat[:, dcol:dcol + 1], stat[:, mcol:mcol + 1], stat[:, pcol:pcol + 1]))
                    wait("scalar", SV, tv)
                    wait("scalar", ST, trans_end[sp])
                    bcol = 29 + (i % 2) * 6
                    t_e = op("scalar", SS, lambda EG, kb=kb, mcol=mcol, sp=sp, bcol=bcol: EG.activation(
                        psb[sp][:, 0:kb], pss[sp][:, 0:kb], AF.Exp, bias=stat[:, mcol:mcol + 1],
                        accum_out=stat[:, bcol:bcol + 1]))
                    if b > 0:
                        acol = 28 + (i % 2) * 5
                        dcol = 27 + (i % 2) * 7
                        t_e = op("scalar", SS, lambda EG, acol=acol, dcol=dcol: EG.activation(
                            stat[:, acol:acol + 1], stat[:, dcol:dcol + 1], AF.Exp))
                    exp_end[sp] = t_e
                    return t_e

                def emit_block_tail1(i):
                    j, b, kb, last = blocks[i]
                    sp = i % 2
                    wait("tensor", SS, exp_end[sp])
                    wait("tensor", SV, ptev_end[sp])
                    for kc2 in range(kb // P):
                        fn = lambda EG, kc2=kc2, sp=sp: EG.transpose(
                            psp[sp][:, kc2 * P:(kc2 + 1) * P], psb[sp][:, kc2 * P:(kc2 + 1) * P], identb[:])
                        if kc2 == kb // P - 1:
                            t_tr = op("tensor", ST, fn)
                        else:
                            quiet("tensor", lambda EG, f=fn: f(EG))
                    trans_end[sp] = t_tr
                    wait("vector", ST, t_tr)
                    wait("vector", ST, av_end[sp])
                    tv_p = op("vector", SV, lambda EG, kb=kb, sp=sp: EG.tensor_copy(
                        pTs[sp][:, 0:kb], psp[sp][:, 0:kb]))
                    ptev_end[sp] = tv_p

                def emit_block_tail2(i):
                    j, b, kb, last = blocks[i]
                    sp = i % 2
                    s = i % 4
                    wait("tensor", SV, ptev_end[sp])
                    wait("tensor", SV, o_prev[0])
                    wait("tensor", S_VB[s], slot_loaded[s])
                    nkc2 = kb // P
                    for dh in range(2):
                        for kc2 in range(nkc2):
                            fn = lambda EG, s=s, u=kc2, dh=dh, kc2=kc2, nkc2=nkc2, sp=sp: EG.matmul(
                                pso[:, dh * 512:(dh + 1) * 512],
                                pTs[sp][:, kc2 * P:(kc2 + 1) * P],
                                vblk[s][:, u * D + dh * 512: u * D + dh * 512 + 512],
                                start=(kc2 == 0), stop=(kc2 == nkc2 - 1))
                            if dh == 1 and kc2 == nkc2 - 1:
                                t_av = op("tensor", ST, fn)
                            else:
                                quiet("tensor", lambda EG, f=fn: f(EG))
                    slot_busy[s] = t_av
                    av_end[sp] = t_av
                    bcol = 29 + (i % 2) * 6
                    scol = 30 + (b % 2)
                    wait("vector", SS, exp_end[i % 2])
                    if b == 0:
                        tv_s = op("vector", SV, lambda EG, scol=scol, bcol=bcol: EG.tensor_copy(
                            stat[:, scol:scol + 1], stat[:, bcol:bcol + 1]))
                    else:
                        acol = 28 + (i % 2) * 5
                        pscol = 30 + ((b - 1) % 2)
                        wait("vector", SV, sums_prev[0])
                        tv_s = op("vector", SV, lambda EG, scol=scol, pscol=pscol, acol=acol, bcol=bcol:
                                  EG.scalar_tensor_tensor(stat[:, scol:scol + 1], stat[:, pscol:pscol + 1],
                                                          stat[:, acol:acol + 1], stat[:, bcol:bcol + 1],
                                                          op0=ALU.mult, op1=ALU.add))
                    sums_prev[0] = tv_s
                    wait("vector", ST, t_av)
                    if b == 0:
                        tv_o = op("vector", SV, lambda EG: EG.tensor_copy(oacc[:], pso[:]))
                    else:
                        acol = 28 + (i % 2) * 5
                        wait("vector", SV, o_prev[0])
                        tv_o = op("vector", SV, lambda EG, acol=acol: EG.scalar_tensor_tensor(
                            oacc[:], oacc[:], stat[:, acol:acol + 1], pso[:], op0=ALU.mult, op1=ALU.add))
                    o_prev[0] = tv_o
                    if last:
                        jpar = j % 2
                        fcol = 30 + (b % 2)
                        wait("vector", SV, tv_s)
                        tv = op("vector", SV, lambda EG, fcol=fcol: EG.reciprocal(
                            stat[:, 32:33], stat[:, fcol:fcol + 1]))
                        wait("vector", SV, tv)
                        wait("vector", S_XT[j % 2], xq_ld[j])
                        wait("vector", S_X2S[jpar], x2s_free[jpar])
                        tv_f = op("vector", SV, lambda EG, jpar=jpar: EG.scalar_tensor_tensor(
                            x2s[jpar][:], oacc[:], stat[:, 32:33], xtile[jpar][:], op0=ALU.mult, op1=ALU.add))
                        xt_free[jpar] = tv_f
                        o_prev[0] = tv_f
                        wait("sync", SV, tv_f)
                        x2s_free[jpar] = dma("sync", S_X2S[jpar], x2_hbm.ap()[j * P:(j + 1) * P, :], x2s[jpar][:])
                        x2st[j] = x2s_free[jpar]

                for i in range(NB):
                    j, b, kb, last = blocks[i]
                    s = i % 4
                    wait("gpsimd", ST, slot_busy[s])
                    slot_loaded[s] = dma("gpsimd", S_VB[s],
                                         vblk[s][:].rearrange("p (u e) -> p u e", e=D)[:, 0:kb // P, :],
                                         v_hbm.ap()[b * 1024: b * 1024 + kb, :]
                                         .rearrange("(u p) e -> p u e", p=P))
                    t_sc = emit_scores(i)
                    if INTERLEAVE:
                        if i > 0:
                            emit_block_tail1(i - 1)
                        emit_softmax_head(i, t_sc)
                        if i > 0:
                            emit_block_tail2(i - 1)
                    else:
                        emit_softmax_head(i, t_sc)
                        emit_block_tail1(i)
                        emit_block_tail2(i)
                    if b == 0 and j + 1 < NT:
                        npar = (j + 1) % 2
                        wait("sync", ST, qts_free[npar])
                        qts_ld[npar] = dma("sync", S_QTS[npar], qts[npar][:],
                                           qT_hbm.ap()[:, (j + 1) * 1024:(j + 2) * 1024])
                        wait("sync", SV, xt_free[npar])
                        xq_ld[j + 1] = dma("sync", S_XT[npar], xtile[npar][:],
                                           xq.ap()[(j + 1) * P:(j + 2) * P, :])
                if INTERLEAVE:
                    emit_block_tail1(NB - 1)
                    emit_block_tail2(NB - 1)
                pl.run(nc)
            es12.close()

            with ExitStack() as es3:
                w1s = es3.enter_context(nc.sbuf_tensor("w1s", [P, DC * I], BF16))
                w3s = es3.enter_context(nc.sbuf_tensor("w3s", [P, DC * I], BF16))
                w2s = es3.enter_context(nc.sbuf_tensor("w2s", [P, ICN * D], BF16))
                ln2T = es3.enter_context(nc.sbuf_tensor("ln2T", [P, DC * 2048], BF16))
                hsb = [es3.enter_context(nc.sbuf_tensor(f"hsb{i}", [P, ICN * 512], BF16)) for i in range(2)]
                sils = [es3.enter_context(nc.sbuf_tensor(f"sils{i}", [P, 512], F32)) for i in range(2)]
                ytmp = [es3.enter_context(nc.sbuf_tensor(f"ytmp{i}", [P, D], F32)) for i in range(2)]
                sqs = es3.enter_context(nc.sbuf_tensor("sqs3", [P, D], F32))
                lnb3 = es3.enter_context(nc.sbuf_tensor("lnb3", [P, 2 * D], BF16))
                x4 = [es3.enter_context(nc.sbuf_tensor(f"x4_{i}", [P, 2 * D], F32)) for i in range(2)]
                cst = es3.enter_context(nc.sbuf_tensor("cst", [P, 112], F32))

                for _eng in ("vector", "scalar", "sync", "gpsimd"):
                    wait(_eng, S_X2S[0], S_X2S[0].n)
                    wait(_eng, S_X2S[1], S_X2S[1].n)

                with ExitStack() as es3a:
                    ptr3 = es3a.enter_context(nc.psum_tensor("ptr3", [P, D], BF16))
                    psr = es3a.enter_context(nc.psum_tensor("psr", [P, E], F32))

                    wait("tensor", S_CONST, t_ldf)
                    t_br = op("tensor", ST, lambda EG: EG.matmul(psr[:], onesf[:], brrow[:], start=True, stop=True))
                    wait("vector", ST, t_br)
                    t_brv = op("vector", SV, lambda EG: EG.tensor_copy(brbc[:], psr[:]))

                    x4_free = [0, 0]
                    lnb_busy = 0
                    tr_free = 0
                    router_done = t_brv
                    lnT_evac = 0
                    lnT_tile_done = [0] * NT
                    for g in range(8):
                        par = g % 2
                        wait("sync", SV, x4_free[par])
                        wait("sync", S_X2S[(2 * g) % 2], x2st[2 * g])
                        wait("sync", S_X2S[(2 * g + 1) % 2], x2st[2 * g + 1])
                        tl = dma("sync", S_LT[par], x4[par][:].rearrange("p (t d) -> p t d", d=D),
                                 x2_hbm.ap()[2 * g * P:(2 * g + 2) * P, :].rearrange("(t p) d -> p t d", p=P))
                        wait("vector", S_LT[par], tl)
                        for t in range(2):
                            tv = op("vector", SV, lambda EG, t=t, par=par: EG.reduce_sum(
                                cst[:, t:t + 1], x4[par][:, t * D:(t + 1) * D], axis=AX.X, negate=True))
                        wait("vector", SV, tv)
                        for t in range(2):
                            tv = op("vector", SV, lambda EG, t=t: EG.tensor_scalar_mul(
                                cst[:, 4 + t:5 + t], cst[:, t:t + 1], 1.0 / D))
                        wait("scalar", SV, tv)
                        for t in range(2):
                            if t == 1:
                                wait("scalar", SS, ts_)
                            ts_ = op("scalar", SS, lambda EG, t=t, par=par: EG.activation(
                                sqs[:], x4[par][:, t * D:(t + 1) * D], AF.Square,
                                bias=cst[:, 4 + t:5 + t], accum_out=cst[:, 8 + t:9 + t]))
                        wait("vector", SS, ts_)
                        for t in range(2):
                            tv = op("vector", SV, lambda EG, t=t: EG.tensor_scalar(
                                cst[:, 12 + t:13 + t], cst[:, 8 + t:9 + t], 1.0 / D, EPS,
                                op0=ALU.mult, op1=ALU.add))
                        wait("scalar", SV, tv)
                        for t in range(2):
                            ts_ = op("scalar", SS, lambda EG, t=t: EG.activation(
                                cst[:, 16 + t:17 + t], cst[:, 12 + t:13 + t], AF.Sqrt))
                        wait("vector", SS, ts_)
                        for t in range(2):
                            tv = op("vector", SV, lambda EG, t=t: EG.reciprocal(
                                cst[:, 20 + t:21 + t], cst[:, 16 + t:17 + t]))
                        wait("vector", SV, tv)
                        wait("vector", ST, lnb_busy)
                        for t in range(2):
                            tv = op("vector", SV, lambda EG, t=t, par=par: EG.tensor_scalar(
                                lnb3[:, t * D:(t + 1) * D], x4[par][:, t * D:(t + 1) * D],
                                cst[:, 4 + t:5 + t], cst[:, 20 + t:21 + t], op0=ALU.add, op1=ALU.mult))
                        x4_free[par] = tv
                        norm2 = tv
                        wait("tensor", SV, norm2)
                        for t in range(2):
                            j = 2 * g + t
                            wait("tensor", SV, tr_free)
                            for c in range(DC):
                                fn = lambda EG, t=t, c=c: EG.transpose(
                                    ptr3[:, c * P:(c + 1) * P],
                                    lnb3[:, t * D + c * P: t * D + (c + 1) * P], identb[:])
                                if c == DC - 1:
                                    t_tr = op("tensor", ST, fn)
                                else:
                                    quiet("tensor", lambda EG, f=fn: f(EG))
                            if t == 1:
                                lnb_busy = t_tr
                            wait("vector", ST, t_tr)
                            tv = op("vector", SV, lambda EG, j=j: EG.tensor_copy(
                                ln2T[:].rearrange("p (c v) -> p c v", v=2048)[:, :, j * P:(j + 1) * P],
                                ptr3[:].rearrange("p (c u) -> p c u", u=P)))
                            tr_free = tv
                            lnT_evac = tv
                            lnT_tile_done[j] = tv
                            wait("tensor", SV, tv)
                            wait("tensor", SG, SG_PRE)
                            wait("tensor", SV, router_done)
                            for dc in range(DC):
                                fn = lambda EG, dc=dc, j=j: EG.matmul(
                                    psr[:], ln2T[:, dc * 2048 + j * P: dc * 2048 + (j + 1) * P],
                                    wrh[:, dc * E:(dc + 1) * E],
                                    start=(dc == 0), stop=(dc == DC - 1))
                                if dc == DC - 1:
                                    t_r = op("tensor", ST, fn)
                                else:
                                    quiet("tensor", lambda EG, f=fn: f(EG))
                            wait("vector", ST, t_r)
                            wait("vector", SV, t_brv)
                            tv = op("vector", SV, lambda EG, j=j: EG.tensor_add(
                                logit[:, j * E:(j + 1) * E], psr[:], brbc[:]))
                            router_done = tv

                    wait("vector", SV, router_done)
                    for j in range(NT):
                        tv = op("vector", SV, lambda EG, j=j: EG.reduce_max(
                            cst[:, 32 + j:33 + j], logit[:, j * E:(j + 1) * E], axis=AX.X))
                    wait("vector", SV, tv)
                    tv = op("vector", SV, lambda EG: EG.tensor_scalar(
                        cst[:, 48:64], cst[:, 32:48], -1.0, None, op0=ALU.mult))
                    wait("vector", SV, tv)
                    for j in range(NT):
                        tv = op("vector", SV, lambda EG, j=j: EG.tensor_scalar(
                            csb[:, j * E:(j + 1) * E], logit[:, j * E:(j + 1) * E],
                            cst[:, 32 + j:33 + j], -1e30, op0=ALU.is_equal, op1=ALU.mult))
                    wait("vector", SV, tv)
                    for j in range(NT):
                        tv = op("vector", SV, lambda EG, j=j: EG.tensor_add(
                            csb[:, j * E:(j + 1) * E], csb[:, j * E:(j + 1) * E],
                            logit[:, j * E:(j + 1) * E]))
                    wait("vector", SV, tv)
                    for j in range(NT):
                        tv = op("vector", SV, lambda EG, j=j: EG.reduce_max(
                            cst[:, 64 + j:65 + j], csb[:, j * E:(j + 1) * E], axis=AX.X))
                    wait("vector", SV, tv)
                    tv_d = op("vector", SV, lambda EG: EG.tensor_sub(cst[:, 80:96], cst[:, 64:80], cst[:, 32:48]))
                    wait("scalar", SV, tv_d)
                    for j in range(NT):
                        ts_ = op("scalar", SS, lambda EG, j=j: EG.activation(
                            sqs[:, j * E:(j + 1) * E], logit[:, j * E:(j + 1) * E], AF.Exp,
                            bias=cst[:, 48 + j:49 + j]))
                    ts_d = op("scalar", SS, lambda EG: EG.activation(cst[:, 80:96], cst[:, 80:96], AF.Exp))
                    wait("vector", SS, ts_d)
                    tv = op("vector", SV, lambda EG: EG.tensor_scalar(
                        cst[:, 80:96], cst[:, 80:96], 1.0, None, op0=ALU.add))
                    wait("vector", SV, tv)
                    tv = op("vector", SV, lambda EG: EG.reciprocal(cst[:, 96:112], cst[:, 80:96]))
                    wait("vector", SV, tv)
                    for j in range(NT):
                        tv = op("vector", SV, lambda EG, j=j: EG.tensor_scalar(
                            csb[:, j * E:(j + 1) * E], logit[:, j * E:(j + 1) * E],
                            cst[:, 64 + j:65 + j], None, op0=ALU.is_ge))
                    wait("vector", SV, tv)
                    for j in range(NT):
                        tv = op("vector", SV, lambda EG, j=j: EG.tensor_mul(
                            csb[:, j * E:(j + 1) * E], csb[:, j * E:(j + 1) * E], sqs[:, j * E:(j + 1) * E]))
                    wait("vector", SV, tv)
                    for j in range(NT):
                        tv = op("vector", SV, lambda EG, j=j: EG.tensor_scalar(
                            csb[:, j * E:(j + 1) * E], csb[:, j * E:(j + 1) * E],
                            cst[:, 96 + j:97 + j], None, op0=ALU.mult))
                    c_done = tv

                    es3b = ExitStack()
                    psg = [es3b.enter_context(nc.psum_tensor(f"psg{i}", [P, 512], F32)) for i in range(2)]
                    psu = [es3b.enter_context(nc.psum_tensor(f"psu{i}", [P, 512], F32)) for i in range(2)]
                    psy = [es3b.enter_context(nc.psum_tensor(f"psy{i}", [P, 512], F32)) for i in range(2)]

                    def load_w13_quarter(e, q):
                        dma("gpsimd", S_WQ4[q],
                            w1s[:].rearrange("p (c i) -> p c i", i=I)[:, :, q * 512:(q + 1) * 512],
                            w1.ap()[e][:, q * 512:(q + 1) * 512].rearrange("(c p) i -> p c i", p=P))
                        return dma("gpsimd", S_WQ4[q],
                                   w3s[:].rearrange("p (c i) -> p c i", i=I)[:, :, q * 512:(q + 1) * 512],
                                   w3.ap()[e][:, q * 512:(q + 1) * 512].rearrange("(c p) i -> p c i", p=P))

                    def load_w2(e):
                        return dma("gpsimd", S_W2, w2s[:].rearrange("p (c d) -> p c d", d=D),
                                   w2.ap()[e].rearrange("(c p) d -> p c d", p=P))

                    wq13 = [load_w13_quarter(0, q) for q in range(4)]
                    t_w2 = load_w2(0)

                    psg_free = [0, 0]
                    psu_free = [0, 0]
                    psy_free = [0, 0]
                    sil_free = [0, 0]
                    ytmp_free = [0, 0]
                    h_ready = 0
                    w2_lastp = [0, 0]
                    gate_end = [0] * E
                    w2_end = [0] * E
                    acc_tile = [0] * NT

                    x2_3b = [0] * NT
                    def load_x2_3b(j):
                        wait("sync", SV, x4_free[j % 2])
                        wait("sync", S_X2S[j % 2], x2st[j])
                        x2_3b[j] = dma("sync", S_XT[j % 2], x4[j % 2][:, 0:D],
                                       x2_hbm.ap()[j * P:(j + 1) * P, :])
                    load_x2_3b(0)
                    load_x2_3b(1)
                    q_mark = [0] * 4

                    def emit_w2_phase(e, tb):
                        nonlocal w2_lastp, w2_end, acc_tile, ytmp_free, psy_free, x4_free, t_w2
                        hpar = tb % 2
                        if tb == 0:
                            wait("tensor", S_W2, t_w2)
                        for ts4 in range(4):
                            tsg = 4 * tb + ts4
                            for dh in range(2):
                                ypar = (ts4 * 2 + dh) % 2
                                wait("tensor", SV, psy_free[ypar])
                                for ic in range(ICN):
                                    fn = lambda E_, ic=ic, ts4=ts4, dh=dh, ypar=ypar, hpar=hpar: E_.matmul(
                                        psy[ypar][:], hsb[hpar][:, ic * 512 + ts4 * P: ic * 512 + (ts4 + 1) * P],
                                        w2s[:, ic * D + dh * 512: ic * D + dh * 512 + 512],
                                        start=(ic == 0), stop=(ic == ICN - 1))
                                    if ic == ICN - 1:
                                        t_y = op("tensor", ST, fn)
                                    else:
                                        quiet("tensor", lambda E_, f=fn: f(E_))
                                if ts4 == 3 and dh == 1:
                                    w2_lastp[hpar] = t_y
                                    if tb == TBN - 1:
                                        w2_end[e] = t_y
                                wait("vector", ST, t_y)
                                if dh == 0:
                                    wait("vector", S_YT[ts4 % 2], ytmp_free[ts4 % 2])
                                if e == 0:
                                    if dh == 0:
                                        wait("vector", S_XT[tsg % 2], x2_3b[tsg])
                                    tv = op("vector", SV, lambda E_, tsg=tsg, dh=dh, ypar=ypar, ts4=ts4:
                                            E_.scalar_tensor_tensor(
                                                ytmp[ts4 % 2][:, dh * 512:(dh + 1) * 512],
                                                psy[ypar][:], csb[:, tsg * E: tsg * E + 1],
                                                x4[tsg % 2][:, dh * 512:(dh + 1) * 512],
                                                op0=ALU.mult, op1=ALU.add))
                                else:
                                    tv = op("vector", SV, lambda E_, tsg=tsg, dh=dh, e=e, ypar=ypar, ts4=ts4:
                                            E_.tensor_scalar(ytmp[ts4 % 2][:, dh * 512:(dh + 1) * 512],
                                                             psy[ypar][:], csb[:, tsg * E + e: tsg * E + e + 1],
                                                             None, op0=ALU.mult))
                                psy_free[ypar] = tv
                                if e == 0 and dh == 1:
                                    x4_free[tsg % 2] = tv
                                    if tsg + 2 < NT:
                                        load_x2_3b(tsg + 2)
                            wait("gpsimd", SV, tv)
                            tg = dma("gpsimd", S_YT[ts4 % 2], out.ap()[tsg * P:(tsg + 1) * P, :],
                                     ytmp[ts4 % 2][:],
                                     accum_op=(ALU.add if e > 0 else ALU.bypass))
                            ytmp_free[ts4 % 2] = tg
                            if e == E - 1:
                                acc_tile[tsg] = tg
                        if e + 1 < E and tb == TBN - 1:
                            wait("gpsimd", ST, w2_end[e])
                            t_w2 = load_w2(e + 1)

                    pending = None
                    for e in range(E):
                        for tb in range(TBN):
                            hpar = tb % 2
                            if e == 0:
                                wait("tensor", SV, lnT_tile_done[4 * tb + 3])
                            for ic in range(ICN):
                                gpar = ic % 2
                                if tb == 0 and ic % 4 == 0:
                                    wait("tensor", S_WQ4[ic // 4], wq13[ic // 4])
                                if ic == 2 and pending is not None:
                                    emit_w2_phase(*pending)
                                    pending = None
                                wait("tensor", SS, psg_free[gpar])
                                for dc in range(DC):
                                    fn = lambda E_, dc=dc, ic=ic, gpar=gpar, tb=tb: E_.matmul(
                                        psg[gpar][:], w1s[:, dc * I + ic * P: dc * I + (ic + 1) * P],
                                        ln2T[:, dc * 2048 + tb * 512: dc * 2048 + (tb + 1) * 512],
                                        start=(dc == 0), stop=(dc == DC - 1))
                                    if dc == DC - 1:
                                        t_g = op("tensor", ST, fn)
                                    else:
                                        quiet("tensor", lambda E_, f=fn: f(E_))
                                wait("tensor", SV, psu_free[gpar])
                                for dc in range(DC):
                                    fn = lambda E_, dc=dc, ic=ic, gpar=gpar, tb=tb: E_.matmul(
                                        psu[gpar][:], w3s[:, dc * I + ic * P: dc * I + (ic + 1) * P],
                                        ln2T[:, dc * 2048 + tb * 512: dc * 2048 + (tb + 1) * 512],
                                        start=(dc == 0), stop=(dc == DC - 1))
                                    if dc == DC - 1:
                                        t_u = op("tensor", ST, fn)
                                    else:
                                        quiet("tensor", lambda E_, f=fn: f(E_))
                                if tb == TBN - 1 and ic % 4 == 3:
                                    q_mark[ic // 4] = t_u
                                if ic == ICN - 1 and tb == TBN - 1:
                                    gate_end[e] = t_u
                                wait("scalar", ST, t_g)
                                wait("scalar", SV, sil_free[gpar])
                                t_s = op("scalar", SS, lambda E_, gpar=gpar: E_.activation(
                                    sils[gpar][:], psg[gpar][:], AF.Silu))
                                psg_free[gpar] = t_s
                                wait("vector", SS, t_s)
                                wait("vector", ST, t_u)
                                if ic == 0:
                                    wait("vector", ST, w2_lastp[hpar])
                                tv = op("vector", SV, lambda E_, ic=ic, gpar=gpar, hpar=hpar: E_.tensor_tensor(
                                    hsb[hpar][:, ic * 512:(ic + 1) * 512], sils[gpar][:], psu[gpar][:],
                                    op=ALU.mult))
                                psu_free[gpar] = tv
                                sil_free[gpar] = tv
                                h_ready = tv
                            wait_val = h_ready
                            pending = (e, tb)
                        if e + 1 < E:
                            nxt = []
                            for q in range(4):
                                wait("gpsimd", ST, q_mark[q])
                                nxt.append(load_w13_quarter(e + 1, q))
                            wq13 = nxt
                    wait("tensor", SV, h_ready)
                    emit_w2_phase(*pending)
                    if DEBUG:
                        wait("sync", SV, c_done)
                        dma("sync", SL, dbg_x2.ap(), x2_hbm.ap())
                        dma("sync", SL, dbg_v.ap(), v_hbm.ap())
                        dma("sync", SL, dbg_q.ap(), qT_hbm.ap())
                        dma("sync", SL, dbg_c.ap(), csb[:])
                        dma("sync", SL, dbg_lg.ap(), logit[:])
                    wait("sync", S_YT[0], S_YT[0].n)
                    wait("sync", S_YT[1], S_YT[1].n)
                    wait("sync", SL, SL.n)
                    pl.run(nc)
                    es3b.close()
    return nc


_NC_CACHE = {}


def _get_nc():
    if "nc" not in _NC_CACHE:
        _NC_CACHE["nc"] = build_nc()
    return _NC_CACHE["nc"]


def _make_masks(p):
    m = np.zeros((NT, P, 256), dtype=np.float32)
    for j in range(NT):
        Lj = 256 * (j + 1)
        q_glob = 128 * (2 * j + p) + np.arange(P)[:, None]
        k_glob = (Lj - 256) + np.arange(256)[None, :]
        m[j] = np.where(k_glob <= q_glob, 0.0, -1e9)
    return m


def kernel(L0_attn_X, L0_attn_Wq, L0_attn_bq, L0_attn_Wk, L0_attn_bk, L0_attn_Wv, L0_attn_bv,
           L0_ffn_Wr, L0_ffn_br, L0_ffn_W1, L0_ffn_W3, L0_ffn_W2):
    X = np.ascontiguousarray(np.asarray(L0_attn_X, dtype=np.float32))
    wqT = np.ascontiguousarray(np.asarray(L0_attn_Wq, dtype=np.float32).T)
    wkT = np.ascontiguousarray(np.asarray(L0_attn_Wk, dtype=np.float32).T)
    wvT = np.ascontiguousarray(np.asarray(L0_attn_Wv, dtype=np.float32).T)
    bq = np.ascontiguousarray(np.asarray(L0_attn_bq, dtype=np.float32))
    bk = np.ascontiguousarray(np.asarray(L0_attn_bk, dtype=np.float32))
    bv = np.ascontiguousarray(np.asarray(L0_attn_bv, dtype=np.float32))
    wr = np.ascontiguousarray(np.asarray(L0_ffn_Wr, dtype=np.float32))
    br = np.ascontiguousarray(np.asarray(L0_ffn_br, dtype=np.float32))
    w1 = np.ascontiguousarray(np.asarray(L0_ffn_W1, dtype=np.float32))
    w3 = np.ascontiguousarray(np.asarray(L0_ffn_W3, dtype=np.float32))
    w2 = np.ascontiguousarray(np.asarray(L0_ffn_W2, dtype=np.float32))

    ident = np.eye(P, dtype=np.float32)
    ones1 = np.ones((1, P), dtype=np.float32)
    triu = np.triu(np.ones((P, P), dtype=np.float32), 1)
    onespp = np.ones((P, P), dtype=np.float32)
    siota = np.broadcast_to(np.arange(640, dtype=np.float32)[None, :], (P, 640)).copy()

    nc = _get_nc()
    in_maps = []
    for c in range(8):
        n, p = c // 2, c % 2
        xkv = X[n]
        xq = np.ascontiguousarray(X[n].reshape(32, P, D)[p::2].reshape(2048, D))
        in_maps.append(dict(xq=xq, xkv=xkv, wqT=wqT, wkT=wkT, wvT=wvT, bq=bq, bk=bk, bv=bv,
                            wr=wr, br=br, w1=w1, w3=w3, w2=w2,
                            ident=ident, ones1=ones1, masks=_make_masks(p),
                            triu=triu, onespp=onespp, siota=siota))
    res = run_bass_kernel_spmd(nc, in_maps, core_ids=list(range(8)))
    out = np.empty((4, 4096, D), dtype=np.float32)
    for c in range(8):
        n, p = c // 2, c % 2
        out[n].reshape(32, P, D)[p::2] = res.results[c]["out"].reshape(NT, P, D)
    return out


if __name__ == "__main__":
    nc = build_nc()
    print("built ok")
